# version 01
# baseline (speedup 1.0000x reference)
import contextlib
import numpy as np
import ml_dtypes
import concourse.bass as bass
import concourse.mybir as mybir
from concourse.bass_utils import run_bass_kernel_spmd
from concourse.bass import IndirectOffsetOnAxis

F32 = mybir.dt.float32
BF16 = mybir.dt.bfloat16
I32 = mybir.dt.int32
AF = mybir.ActivationFunctionType
ALU = mybir.AluOpType
AX = mybir.AxisListType
BB = ml_dtypes.bfloat16

T_FULL = 4096
NT = T_FULL // 128
D = 1024
DC = D // 128

ENGS = ("sync", "tensor", "vector", "scalar", "gpsimd")


class _AliasDict(dict):
    def __init__(self, prog):
        super().__init__()
        self._prog = prog

    def __getitem__(self, k):
        return dict.__getitem__(self, self._prog._n(k))

    def __setitem__(self, k, v):
        dict.__setitem__(self, self._prog._n(k), v)

    def get(self, k, default=None):
        return dict.get(self, self._prog._n(k), default)


class Prog:

    def __init__(self, nc, stack):
        self.nc = nc
        self.stack = stack
        self.sched = {e: [] for e in ENGS}
        self.meta = {e: [] for e in ENGS}
        self.sems = {}
        self.alias = {}
        self.cnt = _AliasDict(self)

    def _n(self, name):
        return self.alias.get(name, name)

    def sem(self, name):
        name = self._n(name)
        if name not in self.sems:
            self.sems[name] = self.stack.enter_context(self.nc.semaphore(name))
            dict.__setitem__(self.cnt, name, 0)
        return self.sems[name]

    def emit(self, eng, fn, meta=None):
        self.sched[eng].append(fn)
        if meta is not None:
            meta = (meta[0], self._n(meta[1]), meta[2]) + tuple(meta[3:])
        self.meta.setdefault(eng, []).append(meta)

    def inc(self, name, n=1):
        self.sem(name)
        name = self._n(name)
        dict.__setitem__(self.cnt, name, dict.__getitem__(self.cnt, name) + n)
        return dict.__getitem__(self.cnt, name)

    def wait(self, eng, name, val=None):
        s = self.sem(name)
        v = self.cnt[name] if val is None else val
        self.emit(eng, lambda e: e.wait_ge(s, v), ("wait", name, v))
        return v

    def dma(self, eng, out_ap, in_ap, sem_name, **kw):
        s = self.sem(sem_name)
        v = self.inc(sem_name, 16)
        self.emit(eng, lambda e: e.dma_start(out=out_ap, in_=in_ap, **kw).then_inc(s, 16),
                  ("inc", sem_name, 16))
        return v

    def dmat(self, eng, out_ap, in_ap, sem_name):
        s = self.sem(sem_name)
        v = self.inc(sem_name, 16)
        self.emit(eng, lambda e: e.dma_start_transpose(out_ap, in_ap).then_inc(s, 16),
                  ("inc", sem_name, 16))
        return v

    def op(self, eng, fn, sem_name=None, n=1):
        if sem_name is None:
            self.emit(eng, lambda e: fn(e))
            return None
        s = self.sem(sem_name)
        v = self.inc(sem_name, n)

        def go(e):
            fn(e)
            e.drain(fusable=True).then_inc(s, n)
        self.emit(eng, go, ("inc", sem_name, n))
        return v

    def check_deadlock(self):
        pos = {e: 0 for e in ENGS}
        cnt = {k: 0 for k in self.cnt}
        total = sum(len(v) for v in self.sched.values())
        done = 0
        while done < total:
            progressed = False
            for e in ENGS:
                while pos[e] < len(self.sched[e]):
                    m = self.meta[e][pos[e]]
                    if m is not None and m[0] == "wait" and cnt.get(m[1], 0) < m[2]:
                        break
                    if m is not None and m[0] == "inc":
                        cnt[m[1]] = cnt.get(m[1], 0) + m[2]
                    pos[e] += 1
                    done += 1
                    progressed = True
            if not progressed:
                heads = {e: (pos[e], self.meta[e][pos[e]] if pos[e] < len(self.sched[e]) else "END")
                         for e in ENGS}
                raise RuntimeError(f"DEADLOCK: engine heads {heads}; sems {dict(list(cnt.items())[:40])}")
        return True

    def realize(self):
        with self.nc.Block() as block:
            block.sync(lambda e: [f(e) for f in self.sched["sync"]])
            block.tensor(lambda e: [f(e) for f in self.sched["tensor"]])
            block.vector(lambda e: [f(e) for f in self.sched["vector"]])
            block.scalar(lambda e: [f(e) for f in self.sched["scalar"]])
            block.gpsimd(lambda e: [f(e) for f in self.sched["gpsimd"]])



def alloc_ln(nc, stack, tag="", with_xlnT=True):
    c = {}
    c["xrot"] = stack.enter_context(nc.sbuf_tensor(f"xrot{tag}", [128, 4 * D], F32))
    c["lrot"] = stack.enter_context(nc.sbuf_tensor(f"lrot{tag}", [128, 4 * D], BF16))
    c["lnstat"] = stack.enter_context(nc.sbuf_tensor(f"lnstat{tag}", [128, 4 * NT], F32))
    c["sq"] = stack.enter_context(nc.sbuf_tensor(f"sq{tag}", [128, D], F32))
    if with_xlnT:
        c["xlnT"] = stack.enter_context(nc.sbuf_tensor(f"xlnT{tag}", [128, DC * T_FULL], BF16))
    return c


def xlnT_chunk(c, ch):
    return c["xlnT"][:, ch * T_FULL:(ch + 1) * T_FULL]


def xlnT_tile(c, ch, t):
    return c["xlnT"][:, ch * T_FULL + t * 128: ch * T_FULL + (t + 1) * 128]


def ln_mean(c, t):
    return c["lnstat"][:, 2 * t:2 * t + 1]


def ln_rstd(c, t):
    return c["lnstat"][:, 2 * t + 1:2 * t + 2]


def emit_outer_ln(p, c, x_in, scr_bf, tag, transpose=True, scrF=None):
    xsrc = x_in.rearrange("(t p) d -> t p d", p=128)
    ldst = scr_bf.rearrange("(t p) d -> t p d", p=128)
    s_bn = f"{tag}_bn"
    s_rstd = f"{tag}_rstd"
    s_app = f"{tag}_app"
    for t in range(NT):
        r = t % 4
        xt = c["xrot"][:, r * D:(r + 1) * D]
        s_ld = f"{tag}_ld{r}"
        bs = c["sq"][:, t * 12:(t + 1) * 12]
        if t >= 4:
            p.wait("sync", s_bn, 2 * (t - 4) + 1)
        vld = p.dma("sync", xt, xsrc[t], s_ld)
        p.wait("vector", s_ld, vld)
        p.op("vector", lambda e, xt=xt, bs=bs: e.bn_stats(bs[:, 0:6], xt[:, 0:512]))
        p.op("vector", lambda e, xt=xt, bs=bs: e.bn_stats(bs[:, 6:12], xt[:, 512:1024]), s_bn)
        p.wait("vector", s_bn, p.cnt[s_bn])
        p.op("vector", lambda e, t=t, bs=bs: e.bn_aggr(c["lnstat"][:, 2 * t:2 * t + 2], bs), s_bn)
    rv = c["lnstat"][:, 0:2 * NT].rearrange("p (t two) -> p two t", two=2)[:, 1]
    p.wait("vector", s_bn)
    p.op("vector", lambda e: e.tensor_scalar(rv, rv, 1e-5, None, ALU.add), s_rstd)
    p.wait("scalar", s_rstd, p.cnt[s_rstd])
    p.op("scalar", lambda e: e.activation(rv, rv, AF.Ln), s_rstd)
    p.wait("scalar", s_rstd, p.cnt[s_rstd])
    p.op("scalar", lambda e: e.activation(rv, rv, AF.Exp, scale=-0.5), s_rstd)
    p.wait("vector", s_rstd)
    p.wait("sync", s_bn)
    for t in range(NT):
        r = t % 4
        xt = c["xrot"][:, r * D:(r + 1) * D]
        lt = c["lrot"][:, r * D:(r + 1) * D]
        s_ld = f"{tag}_ld{r}"
        s_st = f"{tag}_st{r}"
        napp = 2 if scrF is not None else 1
        if t >= 4:
            p.wait("vector", s_st, p.cnt[s_st])
            p.wait("sync", s_app, napp * (t - 3))
        vld = p.dma("sync", xt, xsrc[t], s_ld)
        p.wait("vector", s_ld, vld)
        p.op("vector", lambda e, t=t, xt=xt, lt=lt: e.tensor_scalar(
            lt, xt, ln_mean(c, t), ln_rstd(c, t), ALU.subtract, ALU.mult), s_app)
        p.wait("sync", s_app, p.cnt[s_app])
        p.dma("sync", ldst[t], lt, s_st)
        if scrF is not None:
            fdst = scrF.rearrange("(t p) d -> t p d", p=128)
            ft = c["lrotf"][:, (t % 2) * D:((t % 2) + 1) * D]
            s_stf = f"{tag}_stf{t % 2}"
            if t >= 2:
                p.wait("vector", s_stf, p.cnt[s_stf])
            p.op("vector", lambda e, t=t, xt=xt, ft=ft: e.tensor_scalar(
                ft, xt, ln_mean(c, t), ln_rstd(c, t), ALU.subtract, ALU.mult), s_app)
            p.wait("sync", s_app, p.cnt[s_app])
            p.dma("sync", fdst[t], ft, s_stf)
    for r in range(4):
        p.wait("sync", f"{tag}_st{r}")
    if scrF is not None:
        for r in range(2):
            p.wait("sync", f"{tag}_stf{r}")
    if transpose:
        for ch in range(DC):
            p.dmat("sync", xlnT_chunk(c, ch), scr_bf[:, ch * 128:(ch + 1) * 128], f"{tag}_tp")
    return f"{tag}_tp"


_RE_LAST4_NOTE = """re-apply order: tiles 0..NT-5 are re-loaded; last 4 reuse resident slots."""


def emit_residual_tile(p, c, x_in, x_out, t, psum_aps, tag, s_ready, ready_val,
                       guard=None):
    r = t % 4
    xt = c["xrot"][:, r * D:(r + 1) * D]
    s_ld = f"{tag}_rld{r}"
    s_ev = f"{tag}_rev"
    s_st = f"{tag}_rst{r}"
    nh = len(psum_aps)
    w = D // nh
    if t < 4 and guard is not None:
        p.wait("sync", guard[0], guard[1])
    if t >= 4:
        p.wait("sync", s_st, p.cnt[s_st])
        p.wait("sync", s_ev, nh * (t - 3))
    vld = p.dma("sync", xt, x_in.rearrange("(t p) d -> t p d", p=128)[t], s_ld)
    p.wait("vector", s_ld, vld)
    p.wait("vector", s_ready, ready_val)
    for h, ps in enumerate(psum_aps):
        p.op("vector", lambda e, h=h, ps=ps, xt=xt, w=w: e.tensor_tensor(
            xt[:, h * w:(h + 1) * w], ps, xt[:, h * w:(h + 1) * w], ALU.add), s_ev)
    p.wait("sync", s_ev, p.cnt[s_ev])
    p.dma("sync", x_out.rearrange("(t p) d -> t p d", p=128)[t], xt, s_st)
    return s_ev


def finish_store(p, tag):
    for r in range(4):
        p.wait("sync", f"{tag}_rst{r}")



def emit_l0_ffn(p, nc, c, stack, W_dram, scrW_dram, x_in, x_out, tag):
    wrot = stack.enter_context(nc.sbuf_tensor(f"{tag}_wrot", [128, 2 * D], F32))
    srot = stack.enter_context(nc.sbuf_tensor(f"{tag}_srot", [128, 2 * D], BF16))
    sgT = stack.enter_context(nc.sbuf_tensor(f"{tag}_sgT", [128, DC * D], BF16))
    ps = [stack.enter_context(nc.psum_tensor(f"{tag}_ps{i}", [128, 512], F32)) for i in range(4)]

    wsrc = W_dram.rearrange("(ch p) i -> ch p i", p=128)
    ssc = scrW_dram.rearrange("(ch p) i -> ch p i", p=128)
    for ch in range(DC):
        r = ch % 2
        s_wl = f"{tag}_wl{r}"
        s_sw = f"{tag}_sw{r}"
        if ch >= 2:
            p.wait("sync", f"{tag}_sg", ch - 1)
        vw = p.dma("sync", wrot[:, r * D:(r + 1) * D], wsrc[ch], s_wl)
        p.wait("scalar", s_wl, vw)
        if ch >= 2:
            p.wait("scalar", s_sw, p.cnt[s_sw])
        p.op("scalar", lambda e, r=r: e.sign(srot[:, r * D:(r + 1) * D], wrot[:, r * D:(r + 1) * D]), f"{tag}_sg")
        p.wait("sync", f"{tag}_sg", p.cnt[f"{tag}_sg"])
        p.dma("sync", ssc[ch], srot[:, r * D:(r + 1) * D], s_sw)
    p.wait("sync", f"{tag}_sw0")
    p.wait("sync", f"{tag}_sw1")
    for ch in range(DC):
        p.dmat("sync", sgT[:, ch * D:(ch + 1) * D], scrW_dram[:, ch * 128:(ch + 1) * 128], f"{tag}_tp")

    p.wait("tensor", f"{tag}_tp")
    p.wait("tensor", "ln_tp")
    s_mm = f"{tag}_mm"
    for t in range(NT):
        b = (t % 2) * 2
        if t >= 2:
            p.wait("tensor", f"{tag}_rev", 2 * (t - 1))
        for half in range(2):
            for ch in range(DC):
                p.op("tensor", lambda e, t=t, half=half, ch=ch, b=b: e.matmul(
                    ps[b + half][:],
                    xlnT_tile(c, ch, t),
                    sgT[:, ch * D + half * 512: ch * D + (half + 1) * 512],
                    start=(ch == 0), stop=(ch == DC - 1)),
                    s_mm if ch == DC - 1 else None)
        emit_residual_tile(p, c, x_in, x_out, t,
                           [ps[b][:], ps[b + 1][:]], tag, s_mm, p.cnt[s_mm],
                           guard=("ln_app", NT))
    finish_store(p, tag)


def build_l0_ffn(debug=False):
    nc = bass.Bass("TRN2", target_bir_lowering=False)
    x_in = nc.dram_tensor("x_in", [T_FULL, D], F32, kind="ExternalInput").ap()
    W = nc.dram_tensor("W", [D, D], F32, kind="ExternalInput").ap()
    x_out = nc.dram_tensor("x_out", [T_FULL, D], F32, kind="ExternalOutput").ap()
    scr = nc.dram_tensor("scr", [T_FULL, D], BF16, kind="Internal").ap()
    scrW = nc.dram_tensor("scrW", [D, D], BF16, kind="Internal").ap()
    if debug:
        dbg_xlnT = nc.dram_tensor("dbg_xlnT", [128, DC * T_FULL], BF16, kind="ExternalOutput").ap()
        dbg_scr = nc.dram_tensor("dbg_scr", [T_FULL, D], BF16, kind="ExternalOutput").ap()
    with contextlib.ExitStack() as stack:
        p = Prog(nc, stack)
        c = alloc_ln(nc, stack)
        emit_outer_ln(p, c, x_in, scr, "ln")
        emit_l0_ffn(p, nc, c, stack, W, scrW, x_in, x_out, "ffn")
        if debug:
            p.wait("gpsimd", "ffn_rev")
            p.dma("gpsimd", dbg_xlnT, c["xlnT"][:], "dbgs")
            p.dma("gpsimd", dbg_scr, scr, "dbgs")
            p.wait("gpsimd", "dbgs")
        p.realize()
    return nc


_prog_cache = {}


def _get(name, builder):
    if name not in _prog_cache:
        _prog_cache[name] = builder()
    return _prog_cache[name]


def kernel(**inputs):
    B = 8
    cores = list(range(B))
    x = inputs["L0_attn_a_x"]
    xs = [np.ascontiguousarray(x[b], np.float32) for b in range(B)]

    def run(nc, in_maps):
        res = run_bass_kernel_spmd(nc, in_maps, core_ids=cores)
        return [res.results[b] for b in range(B)]

    def A(v):
        return np.ascontiguousarray(np.asarray(v))

    F = lambda n: np.ascontiguousarray(inputs[n], np.float32)
    ones1b = np.ones((1, 128), BB)

    g = {k[len("L0_attn_a_"):]: np.asarray(inputs[k], dtype=np.float32)
         for k in inputs if k.startswith("L0_attn_a_") and k != "L0_attn_a_x"}
    d1, d2, d3 = a0_host_inputs(g)
    nc1 = _get("a0p1", build_l0_attn_a_p1)
    nc2 = _get("a0p2", build_l0_attn_a_p2)
    nc3 = _get("a0p3", build_l0_attn_a_p3)
    r1 = run(nc1, [dict(d1, x_in=xs[b]) for b in range(B)])
    r2 = run(nc2, [dict(d2, ridx_i=A(r1[b]["ridx_o"]), lnstat_i=A(r1[b]["lnstat_o"]))
                   for b in range(B)])
    r3 = run(nc3, [dict(d3, x_in=xs[b], scrF=A(r1[b]["scrF"]),
                        wtile_i=A(r1[b]["wtile_o"]), idxd2=A(r2[b]["idxd2"]))
                   for b in range(B)])
    xs = [A(r3[b]["x_out"]) for b in range(B)]

    nc_ffn = _get("l0_ffn", build_l0_ffn)
    W = F("L0_ffn_weight")
    rf = run(nc_ffn, [{"x_in": xs[b], "W": W} for b in range(B)])
    xs = [A(rf[b]["x_out"]) for b in range(B)]

    cosT, sinT, rotS, mloc, mglob = l0b_consts()
    im_b = dict(
        Wq=F("L0_attn_b_Wq"), Wk=F("L0_attn_b_Wk"), Wv=F("L0_attn_b_Wv"),
        Wo=F("L0_attn_b_Wo"),
        bq_cols=np.ascontiguousarray(F("L0_attn_b_bq").reshape(8, 128).T),
        bk_cols=np.ascontiguousarray(F("L0_attn_b_bk").reshape(2, 128).T),
        bv_row=F("L0_attn_b_bv").reshape(1, 256),
        bo_row=F("L0_attn_b_bo").reshape(1, 1024),
        hw_cols=np.ascontiguousarray(
            np.repeat(F("L0_attn_b_head_weights"), 64).reshape(8, 128).T),
        cosT=cosT, sinT=sinT, rotS=rotS, mloc=mloc, mglob=mglob, ones1=ones1b)
    nc_b = _get("l0b", build_l0_attn_b)
    rb = run(nc_b, [dict(im_b, x_in=xs[b]) for b in range(B)])
    xs = [A(rb[b]["x_out"]) for b in range(B)]

    rf = run(nc_ffn, [{"x_in": xs[b], "W": W} for b in range(B)])
    xs = [A(rf[b]["x_out"]) for b in range(B)]

    mskl, msks = l1a_masks()
    ab = F("L1_attn_a_c_attn_b")
    im_1a = dict(
        attn_w=F("L1_attn_a_c_attn_w"),
        attn_b_row=ab.reshape(1, 3072),
        attn_b_cols=np.ascontiguousarray(ab.reshape(24, 128).T),
        proj_w=F("L1_attn_a_c_proj_w"),
        proj_b_row=F("L1_attn_a_c_proj_b").reshape(1, 1024),
        mskloc=mskl, mskstr=msks, ones1=ones1b)
    nc_1a = _get("l1a", build_l1_attn_a)
    ra = run(nc_1a, [dict(im_1a, x_in=xs[b]) for b in range(B)])
    xs = [A(ra[b]["x_out"]) for b in range(B)]

    im_f1 = dict(
        phi_w=F("L1_ffn_phi_w"),
        phi_b_row=F("L1_ffn_phi_b").reshape(1, 2048),
        expert_w=F("L1_ffn_expert_w"),
        expert_b=F("L1_ffn_expert_b"),
        ones1=ones1b, ident=np.eye(128, dtype=np.float32))
    nc_f1 = _get("l1f", build_l1_ffn)
    rF = run(nc_f1, [dict(im_f1, x_in=xs[b]) for b in range(B)])
    xs = [A(rF[b]["x_out"]) for b in range(B)]

    Lm, Um = l1b_consts()

    def cw_cols(a):
        out = np.empty((128, 32), np.float32)
        for oc in range(8):
            for i in range(4):
                out[:, oc * 4 + i] = a[i, oc * 128:(oc + 1) * 128]
        return out
    im_d1 = dict(
        Wq=F("L1_attn_b_Wq"), Wk=F("L1_attn_b_Wk"), Wv=F("L1_attn_b_Wv"),
        Wb=F("L1_attn_b_Wb"), Wo=F("L1_attn_b_Wo"),
        cq_cols=cw_cols(F("L1_attn_b_conv_q")),
        ck_cols=cw_cols(F("L1_attn_b_conv_k")),
        cv_cols=cw_cols(F("L1_attn_b_conv_v")),
        g_rms_row=F("L1_attn_b_g_rms").reshape(1, 256),
        lng_row=F("L1_attn_b_ln_g").reshape(1, 1024),
        lnb_row=F("L1_attn_b_ln_b").reshape(1, 1024),
        ones1=ones1b, onescol=np.ones((128, 1), BB),
        identb=np.eye(128, dtype=BB), Lmask=Lm, Umask=Um)
    nc_d1 = _get("l1d", build_l1_attn_b)
    rd = run(nc_d1, [dict(im_d1, x_in=xs[b]) for b in range(B)])
    xs = [A(rd[b]["x_out"]) for b in range(B)]

    rF = run(nc_f1, [dict(im_f1, x_in=xs[b]) for b in range(B)])
    xs = [A(rF[b]["x_out"]) for b in range(B)]

    return np.stack(xs).astype(np.float32, copy=False)



def emit_l1_attn_a(p, nc, c, stack, dr, tag="a1"):
    NB = 8
    H = 16
    VS = 66

    kstrT = stack.enter_context(nc.sbuf_tensor(f"{tag}_kstrT", [128, DC * 128], BF16))
    vstr = stack.enter_context(nc.sbuf_tensor(f"{tag}_vstr", [128, H * VS], BF16))
    mskl = stack.enter_context(nc.sbuf_tensor(f"{tag}_mskl", [128, 4 * 512], F32))
    msks = stack.enter_context(nc.sbuf_tensor(f"{tag}_msks", [128, NB * 512], F32))
    bias_c = stack.enter_context(nc.sbuf_tensor(f"{tag}_biasc", [128, 24], F32))
    bias_r = stack.enter_context(nc.sbuf_tensor(f"{tag}_biasr", [1, 3072], BF16))
    pbias_r = stack.enter_context(nc.sbuf_tensor(f"{tag}_pbiasr", [1, D], BF16))
    ones1 = stack.enter_context(nc.sbuf_tensor(f"{tag}_ones1", [1, 128], BF16))
    projw = stack.enter_context(nc.sbuf_tensor(f"{tag}_projw", [128, DC * D], BF16))
    qrot = stack.enter_context(nc.sbuf_tensor(f"{tag}_qrot", [128, 4 * 1056], BF16))
    zr = stack.enter_context(nc.sbuf_tensor(f"{tag}_zr", [128, 8], F32))

    p.dma("sync", mskl[:], dr["mskloc"], f"{tag}_cl")
    p.dma("sync", msks[:], dr["mskstr"], f"{tag}_cl")
    p.dma("sync", bias_c[:], dr["attn_b_cols"], f"{tag}_cl")
    p.dma("gpsimd", bias_r[:], dr["attn_b_row"], f"{tag}_clg")
    p.dma("gpsimd", pbias_r[:], dr["proj_b_row"], f"{tag}_clg")
    p.dma("sync", ones1[:], dr["ones1"], f"{tag}_cl")
    p.dma("gpsimd", projw[:].rearrange("p (ch n) -> p ch n", n=D),
          dr["proj_w"].rearrange("(ch p) n -> p ch n", p=128), f"{tag}_clg")

    with contextlib.ExitStack() as pstack:
        c["xlnT"] = pstack.enter_context(nc.sbuf_tensor(f"{tag}_xlnT", [128, DC * T_FULL], BF16))
        attw = pstack.enter_context(nc.sbuf_tensor(f"{tag}_attw", [128, DC * 3072], BF16))
        ps_a = pstack.enter_context(nc.psum_tensor(f"{tag}_psA", [128, 512], F32))
        ps_b = pstack.enter_context(nc.psum_tensor(f"{tag}_psB", [128, 512], F32))
        ps_v = pstack.enter_context(nc.psum_tensor(f"{tag}_psV", [128, 1024], F32))
        emit_outer_ln(p, c, dr["x_in"], dr["scr"], "ln")

        p.dma("gpsimd", attw[:].rearrange("p (ch n) -> p ch n", n=3072),
              dr["attn_w"].rearrange("(ch p) n -> p ch n", p=128), f"{tag}_wl")
        p.wait("tensor", f"{tag}_wl")
        p.wait("tensor", f"{tag}_cl")
        p.wait("tensor", f"{tag}_clg")
        p.wait("tensor", "ln_tp")
        p.wait("vector", f"{tag}_cl")

        def attw_ch(ch, lo, hi):
            return attw[:, ch * 3072 + lo: ch * 3072 + hi]

        s_ev = f"{tag}_pev"
        for oc in range(16):
            isq = oc < 8
            col0 = (oc % 8) * 128 if isq else 1024 + (oc % 8) * 128
            dstT = dr["qTd"] if isq else dr["kTd"]
            for tg in range(8):
                idx = oc * 8 + tg
                ps = ps_a if idx % 2 == 0 else ps_b
                if idx >= 2:
                    p.wait("tensor", s_ev, idx - 1)
                for ch in range(DC):
                    p.op("tensor", lambda e, ch=ch, col0=col0, tg=tg, ps=ps: e.matmul(
                        ps[:], attw_ch(ch, col0, col0 + 128),
                        xlnT_chunk(c, ch)[:, tg * 512:(tg + 1) * 512],
                        start=(ch == 0), stop=(ch == DC - 1)),
                        f"{tag}_pmm" if ch == DC - 1 else None)
                p.wait("vector", f"{tag}_pmm", p.cnt[f"{tag}_pmm"])
                rr = idx % 4
                rslot = qrot[:, rr * 1056: rr * 1056 + 512]
                s_qst = f"{tag}_qst{rr}"
                if idx >= 4:
                    p.wait("vector", s_qst, p.cnt[s_qst])
                bc = bias_c[:, oc:oc + 1] if isq else bias_c[:, 8 + (oc % 8):9 + (oc % 8)]
                sc = 0.125 if isq else 1.0
                p.op("vector", lambda e, ps=ps, rslot=rslot, bc=bc, sc=sc: e.tensor_scalar(
                    rslot, ps[:], bc, sc, ALU.add, ALU.mult), s_ev)
                p.wait("sync", s_ev, p.cnt[s_ev])
                p.dma("sync",
                      dstT.rearrange("(ch pp) t -> ch pp t", pp=128)[oc % 8, :, tg * 512:(tg + 1) * 512],
                      rslot, s_qst)
        s_vev = f"{tag}_vev"
        for t in range(NT):
            if t >= 1:
                p.wait("tensor", s_vev, 3 * t - 2)
            for half in range(2):
                for ch in range(DC):
                    p.op("tensor", lambda e, ch=ch, t=t, half=half: e.matmul(
                        ps_v[:, half * 512:(half + 1) * 512],
                        xlnT_tile(c, ch, t),
                        attw_ch(ch, 2048 + half * 512, 2048 + (half + 1) * 512),
                        start=(ch == 0), stop=False), None)
                p.op("tensor", lambda e, half=half: e.matmul(
                    ps_v[:, half * 512:(half + 1) * 512], ones1[:],
                    bias_r[:, 2048 + half * 512: 2048 + (half + 1) * 512],
                    start=False, stop=True),
                    f"{tag}_vmm" if half == 1 else None)
            p.wait("vector", f"{tag}_vmm", p.cnt[f"{tag}_vmm"])
            rr = t % 4
            vslot = qrot[:, rr * 1056: (rr + 1) * 1056]
            s_vst = f"{tag}_vst{rr}"
            if t < 4:
                p.wait("vector", f"{tag}_qst{rr}", p.cnt[f"{tag}_qst{rr}"])
            else:
                p.wait("vector", s_vst, p.cnt[s_vst])
            p.op("vector", lambda e, vslot=vslot: e.tensor_copy(
                vslot.rearrange("p (h w) -> p h w", w=VS)[:, :, 0:64],
                ps_v[:].rearrange("p (h w) -> p h w", w=64)), s_vev)
            p.op("vector", lambda e, vslot=vslot: e.memset(
                vslot.rearrange("p (h w) -> p h w", w=VS)[:, :, 65:66], 0.0), s_vev)
            p.op("vector", lambda e, vslot=vslot: e.memset(
                vslot.rearrange("p (h w) -> p h w", w=VS)[:, :, 64:65], 1.0), s_vev)
            p.wait("sync", s_vev, p.cnt[s_vev])
            p.dma("sync", dr["Vd"].rearrange("(t pp) w -> t pp w", pp=128)[t], vslot, s_vst)
        p.wait("tensor", s_ev)
        p.wait("tensor", s_vev)
        for oc in range(8):
            if oc >= 1:
                p.wait("tensor", f"{tag}_sev", oc)
            for ch in range(DC):
                p.op("tensor", lambda e, ch=ch, oc=oc: e.matmul(
                    ps_a[:, 0:128], attw_ch(ch, 1024 + oc * 128, 1024 + (oc + 1) * 128),
                    xlnT_chunk(c, ch).rearrange("p (j s) -> p s j", s=32)[:, 31],
                    start=(ch == 0), stop=(ch == DC - 1)),
                    f"{tag}_smm" if ch == DC - 1 else None)
            p.wait("vector", f"{tag}_smm", p.cnt[f"{tag}_smm"])
            p.op("vector", lambda e, oc=oc: e.tensor_scalar(
                kstrT[:, oc * 128:(oc + 1) * 128], ps_a[:, 0:128],
                bias_c[:, 8 + oc:9 + oc], None, ALU.add), f"{tag}_sev")
        for half in range(2):
            for ch in range(DC):
                p.op("tensor", lambda e, ch=ch, half=half: e.matmul(
                    ps_v[:, half * 512:(half + 1) * 512],
                    xlnT_chunk(c, ch).rearrange("p (j s) -> p s j", s=32)[:, 31],
                    attw_ch(ch, 2048 + half * 512, 2048 + (half + 1) * 512),
                    start=(ch == 0), stop=False), None)
            p.op("tensor", lambda e, half=half: e.matmul(
                ps_v[:, half * 512:(half + 1) * 512], ones1[:],
                bias_r[:, 2048 + half * 512: 2048 + (half + 1) * 512],
                start=False, stop=True), f"{tag}_smm" if half == 1 else None)
        p.wait("vector", f"{tag}_smm", p.cnt[f"{tag}_smm"])
        p.op("vector", lambda e: e.tensor_copy(
            vstr[:].rearrange("p (h w) -> p h w", w=VS)[:, :, 0:64],
            ps_v[:].rearrange("p (h w) -> p h w", w=64)), f"{tag}_sev")
        p.op("vector", lambda e: e.memset(
            vstr[:].rearrange("p (h w) -> p h w", w=VS)[:, :, 65:66], 0.0), f"{tag}_sev")
        p.op("vector", lambda e: e.memset(
            vstr[:].rearrange("p (h w) -> p h w", w=VS)[:, :, 64:65], 1.0), f"{tag}_sev")
        p.wait("vector", f"{tag}_sev")
        p.op("vector", lambda e: e.engine_nop(), f"{tag}_Pdone")
    for r in range(4):
        p.wait("sync", f"{tag}_qst{r}")
        p.wait("sync", f"{tag}_vst{r}")

    qblk = stack.enter_context(nc.sbuf_tensor(f"{tag}_qblk", [128, 2 * DC * 512], BF16))
    kblk = stack.enter_context(nc.sbuf_tensor(f"{tag}_kblk", [128, 2 * DC * 512], BF16))
    vblk = stack.enter_context(nc.sbuf_tensor(f"{tag}_vblk", [128, 2 * 4 * H * VS], BF16))
    Pt = stack.enter_context(nc.sbuf_tensor(f"{tag}_Pt", [128, 2 * 5 * 512], BF16))
    Osb = stack.enter_context(nc.sbuf_tensor(f"{tag}_Osb", [128, 2 * 4 * D], BF16))
    OTsb = stack.enter_context(nc.sbuf_tensor(f"{tag}_OTsb", [128, 2 * DC * 512], BF16))
    ps_s = stack.enter_context(nc.psum_tensor(f"{tag}_psS", [128, 2560], F32))
    ps_o = stack.enter_context(nc.psum_tensor(f"{tag}_psO", [128, 264], F32))
    ps_pr = stack.enter_context(nc.psum_tensor(f"{tag}_psPr", [128, 1024], F32))
    for eng in ("gpsimd", "tensor", "vector", "scalar"):
        p.wait(eng, f"{tag}_Pdone")

    qTd = dr["qTd"].rearrange("(ch pp) t -> ch pp t", pp=128)
    kTd = dr["kTd"].rearrange("(ch pp) t -> ch pp t", pp=128)
    Vdr = dr["Vd"].rearrange("(t pp) w -> t pp w", pp=128)
    s_hd = f"{tag}_hd"

    for n in range(NB):
        rb = n % 2
        s_qb = f"{tag}_qb{rb}"
        s_kb = f"{tag}_kb{rb}"
        s_vb = f"{tag}_vb{rb}"
        qb = qblk[:, rb * DC * 512:(rb + 1) * DC * 512]
        kb = kblk[:, rb * DC * 512:(rb + 1) * DC * 512]
        vb = vblk[:, rb * 4 * H * VS:(rb + 1) * 4 * H * VS]
        if n >= 2:
            p.wait("gpsimd", s_hd, 7 * H * (n - 1))
        for ch in range(DC):
            p.dma("gpsimd", qb[:, ch * 512:(ch + 1) * 512], qTd[ch, :, n * 512:(n + 1) * 512], s_qb)
        for ch in range(DC):
            p.dma("gpsimd", kb[:, ch * 512:(ch + 1) * 512], kTd[ch, :, n * 512:(n + 1) * 512], s_kb)
        for j in range(4):
            p.dma("gpsimd", vb[:, j * H * VS:(j + 1) * H * VS], Vdr[n * 4 + j], s_vb)
        p.wait("tensor", s_qb, p.cnt[s_qb])
        p.wait("tensor", s_kb, p.cnt[s_kb])
        p.wait("tensor", s_vb, p.cnt[s_vb])
        if n >= 2:
            p.wait("vector", f"{tag}_ost{rb}", p.cnt[f"{tag}_ost{rb}"])

        for h in range(H):
            hr = 64 * (h % 2)
            hc = h // 2
            hi = n * H + h
            if hi >= 1:
                p.wait("tensor", s_hd, 7 * (hi - 1) + 3)
            for kc in range(4):
                p.op("tensor", lambda e, kc=kc, hr=hr, hc=hc, kb=kb, qb=qb: e.matmul(
                    ps_s[:, kc * 512:(kc + 1) * 512],
                    kb[hr:hr + 64, hc * 512 + kc * 128: hc * 512 + (kc + 1) * 128],
                    qb[hr:hr + 64, hc * 512:(hc + 1) * 512],
                    start=True, stop=True), None)
            p.op("tensor", lambda e, hr=hr, hc=hc, qb=qb: e.matmul(
                ps_s[:, 2048:2560],
                kstrT[hr:hr + 64, hc * 128:(hc + 1) * 128],
                qb[hr:hr + 64, hc * 512:(hc + 1) * 512],
                start=True, stop=True), f"{tag}_sc")
            p.wait("vector", f"{tag}_sc", p.cnt[f"{tag}_sc"])
            rp = hi % 2
            pt = Pt[:, rp * 2560:(rp + 1) * 2560]
            p.op("vector", lambda e: e.tensor_tensor(
                ps_s[:, 0:2048], ps_s[:, 0:2048], mskl[:], ALU.add), s_hd)
            p.op("vector", lambda e, n=n: e.tensor_tensor(
                ps_s[:, 2048:2560], ps_s[:, 2048:2560],
                msks[:, n * 512:(n + 1) * 512], ALU.add), s_hd)
            p.wait("scalar", s_hd, p.cnt[s_hd])
            if hi >= 2:
                p.wait("scalar", f"{tag}_pv", hi - 1)
            p.op("scalar", lambda e, pt=pt: e.activation(pt, ps_s[:], AF.Exp), s_hd)
            p.wait("tensor", s_hd, p.cnt[s_hd])
            if hi >= 1:
                p.wait("tensor", s_hd, 7 * (hi - 1) + 7)
            for s in range(4):
                for kc in range(4):
                    p.op("tensor", lambda e, s=s, kc=kc, pt=pt, h=h, vb=vb: e.matmul(
                        ps_o[:, s * VS:(s + 1) * VS],
                        pt[:, kc * 512 + s * 128:kc * 512 + (s + 1) * 128],
                        vb[:, kc * H * VS + h * VS: kc * H * VS + (h + 1) * VS],
                        start=(kc == 0), stop=False), None)
                p.op("tensor", lambda e, s=s, pt=pt, h=h: e.matmul(
                    ps_o[:, s * VS:(s + 1) * VS],
                    pt[:, 2048 + s * 128: 2048 + (s + 1) * 128],
                    vstr[:, h * VS:(h + 1) * VS],
                    start=False, stop=True), f"{tag}_pv" if s == 3 else None)
            p.wait("vector", f"{tag}_pv", p.cnt[f"{tag}_pv"])
            for s in range(4):
                p.op("vector", lambda e, s=s: e.reciprocal(
                    zr[:, s:s + 1], ps_o[:, s * VS + 64: s * VS + 65]), None)
            p.op("vector", lambda e: e.engine_nop(), f"{tag}_zz")
            p.wait("vector", f"{tag}_zz", p.cnt[f"{tag}_zz"])
            for s in range(4):
                p.op("vector", lambda e, s=s, h=h, rb=rb: e.tensor_scalar(
                    Osb[:, rb * 4 * D + s * D + h * 64: rb * 4 * D + s * D + (h + 1) * 64],
                    ps_o[:, s * VS: s * VS + 64], zr[:, s:s + 1], None, ALU.mult),
                    s_hd)
        s_ost = f"{tag}_ost{rb}"
        p.wait("sync", s_hd)
        if n >= 2:
            p.wait("sync", s_ost, p.cnt[s_ost])
        for j in range(4):
            p.dma("sync", dr["Od"].rearrange("(t pp) d -> t pp d", pp=128)[n * 4 + j],
                  Osb[:, rb * 4 * D + j * D:(rb * 4 * D) + (j + 1) * D], s_ost)
        p.wait("sync", s_ost, p.cnt[s_ost])
        s_ot = f"{tag}_ot{rb}"
        if n >= 2:
            p.wait("sync", f"{tag}_prmm", 4 * (n - 1))
        for ch in range(DC):
            p.dmat("sync",
                   ot_view(OTsb, rb, ch),
                   dr["Od"][n * 512:(n + 1) * 512, ch * 128:(ch + 1) * 128], s_ot)
        p.wait("tensor", s_ot, p.cnt[s_ot])
        for s in range(4):
            g = n * 4 + s
            if g >= 1:
                p.wait("tensor", f"{tag}_rev", 2 * g)
            for half in range(2):
                for ch in range(DC):
                    p.op("tensor", lambda e, s=s, half=half, ch=ch, rb=rb: e.matmul(
                        ps_pr[:, half * 512:(half + 1) * 512],
                        ot_view(OTsb, rb, ch)[:, s * 128:(s + 1) * 128],
                        projw[:, ch * D + half * 512: ch * D + (half + 1) * 512],
                        start=(ch == 0), stop=False), None)
                p.op("tensor", lambda e, half=half: e.matmul(
                    ps_pr[:, half * 512:(half + 1) * 512], ones1[:],
                    pbias_r[:, half * 512:(half + 1) * 512], start=False, stop=True),
                    f"{tag}_prmm" if half == 1 else None)
            p.wait("vector", f"{tag}_prmm", p.cnt[f"{tag}_prmm"])
            emit_residual_tile(p, c, dr["x_in"], dr["x_out"], g,
                               [ps_pr[:, 0:512], ps_pr[:, 512:1024]], tag,
                               f"{tag}_prmm", p.cnt[f"{tag}_prmm"],
                               guard=("ln_app", NT))
    finish_store(p, tag)


def ot_view(OTsb, rb, ch):
    return OTsb[:, rb * DC * 512 + ch * 512: rb * DC * 512 + (ch + 1) * 512]


def l1a_masks():
    NEG = -10000.0
    k = np.arange(128)
    q = np.arange(512)
    mskl = np.empty((128, 4 * 512), np.float32)
    for cc in range(4):
        mskl[:, cc * 512:(cc + 1) * 512] = np.where(
            (cc * 128 + k)[:, None] <= q[None, :], 0.0, NEG)
    msks = np.empty((128, 8 * 512), np.float32)
    s = np.arange(128)
    for n in range(8):
        msks[:, n * 512:(n + 1) * 512] = np.where(
            (32 * s + 31)[:, None] <= (n * 512 + q)[None, :], 0.0, NEG)
    return mskl, msks


def build_l1_attn_a():
    nc = bass.Bass("TRN2", target_bir_lowering=False)
    x_in = nc.dram_tensor("x_in", [T_FULL, D], F32, kind="ExternalInput").ap()
    attn_w = nc.dram_tensor("attn_w", [D, 3072], F32, kind="ExternalInput").ap()
    attn_b_row = nc.dram_tensor("attn_b_row", [1, 3072], F32, kind="ExternalInput").ap()
    attn_b_cols = nc.dram_tensor("attn_b_cols", [128, 24], F32, kind="ExternalInput").ap()
    proj_w = nc.dram_tensor("proj_w", [D, D], F32, kind="ExternalInput").ap()
    proj_b_row = nc.dram_tensor("proj_b_row", [1, D], F32, kind="ExternalInput").ap()
    mskloc = nc.dram_tensor("mskloc", [128, 2048], F32, kind="ExternalInput").ap()
    mskstr = nc.dram_tensor("mskstr", [128, 4096], F32, kind="ExternalInput").ap()
    ones1 = nc.dram_tensor("ones1", [1, 128], BF16, kind="ExternalInput").ap()
    x_out = nc.dram_tensor("x_out", [T_FULL, D], F32, kind="ExternalOutput").ap()
    scr = nc.dram_tensor("scr", [T_FULL, D], BF16, kind="Internal").ap()
    qTd = nc.dram_tensor("qTd", [D, T_FULL], BF16, kind="Internal").ap()
    kTd = nc.dram_tensor("kTd", [D, T_FULL], BF16, kind="Internal").ap()
    Vd = nc.dram_tensor("Vd", [T_FULL, 16 * 66], BF16, kind="Internal").ap()
    Od = nc.dram_tensor("Od", [T_FULL, D], BF16, kind="Internal").ap()
    with contextlib.ExitStack() as stack:
        p = Prog(nc, stack)
        c = alloc_ln(nc, stack, with_xlnT=False)
        dr = dict(attn_w=attn_w, attn_b_row=attn_b_row, attn_b_cols=attn_b_cols,
                  proj_w=proj_w, proj_b_row=proj_b_row, mskloc=mskloc,
                  mskstr=mskstr, ones1=ones1, x_in=x_in, x_out=x_out, scr=scr,
                  qTd=qTd, kTd=kTd, Vd=Vd, Od=Od)
        emit_l1_attn_a(p, nc, c, stack, dr)
        p.realize()
    return nc



def l0b_consts():
    NEG = -1000000000.0
    nb = 16
    inv = 1.0 / 10000.0 ** (np.arange(nb, dtype=np.float64) / nb)
    ang = np.arange(T_FULL, dtype=np.float64)[:, None] * inv[None, :]
    sin_h = np.repeat(np.sin(ang), 2, axis=-1)
    cos_h = np.repeat(np.cos(ang), 2, axis=-1)
    sin64 = np.concatenate([sin_h, sin_h], -1)
    cos64 = np.concatenate([cos_h, cos_h], -1)
    cosT = np.empty((128, T_FULL), np.float32)
    sinT = np.empty((128, T_FULL), np.float32)
    for p in range(128):
        cosT[p] = cos64[:, p % 64]
        sinT[p] = sin64[:, p % 64]
    S = np.zeros((128, 128), np.float32)
    for m in range(128):
        if m % 64 < 32:
            S[m + 32, m] = -1.0
        else:
            S[m - 32, m] = 1.0
    def mk(t):
        base = min(max(t - 1, 0), 29)
        q = t * 128 + np.arange(128)
        m = np.empty((128, 384), np.float32)
        for cc in range(3):
            k = (base + cc) * 128 + np.arange(128)
            ok = (np.abs(q[None, :] - k[:, None]) <= 12) & (k[:, None] >= 8)
            m[:, cc * 128:(cc + 1) * 128] = np.where(ok.T.T, 0.0, NEG) if False else np.where(ok, 0.0, NEG)
        return m
    mloc = np.stack([mk(0), mk(1), mk(2), mk(31)])
    mglob = np.where(np.arange(128)[:, None] < 8, 0.0, NEG).astype(np.float32)
    mglob = np.broadcast_to(mglob, (128, 128)).copy()
    return (cosT.astype(BB), sinT.astype(BB), S.astype(BB),
            np.ascontiguousarray(mloc.transpose(1, 0, 2).reshape(128, 4 * 384)),
            mglob)


def emit_l0_attn_b(p, nc, c, stack, dr, tag="b0"):
    H = 16
    VS = 66
    KV = 4

    kT = stack.enter_context(nc.sbuf_tensor(f"{tag}_kT", [128, 2 * T_FULL], BF16))
    vsb = stack.enter_context(nc.sbuf_tensor(f"{tag}_vsb", [128, NT * KV * VS], BF16))
    mskl = stack.enter_context(nc.sbuf_tensor(f"{tag}_mskl", [128, 4 * 384], F32))
    mskg = stack.enter_context(nc.sbuf_tensor(f"{tag}_mskg", [128, 128], F32))
    hwq = stack.enter_context(nc.sbuf_tensor(f"{tag}_hwq", [128, 8], F32))
    bqc = stack.enter_context(nc.sbuf_tensor(f"{tag}_bqc", [128, 8], F32))
    bkc = stack.enter_context(nc.sbuf_tensor(f"{tag}_bkc", [128, 2], F32))
    bvr = stack.enter_context(nc.sbuf_tensor(f"{tag}_bvr", [1, 256], BF16))
    bor = stack.enter_context(nc.sbuf_tensor(f"{tag}_bor", [1, D], BF16))
    ones1 = stack.enter_context(nc.sbuf_tensor(f"{tag}_ones1", [1, 128], BF16))
    projw = stack.enter_context(nc.sbuf_tensor(f"{tag}_projw", [128, DC * D], BF16))
    rotS = stack.enter_context(nc.sbuf_tensor(f"{tag}_rotS", [128, 128], BF16))

    p.dma("sync", mskl[:], dr["mloc"], f"{tag}_cl")
    p.dma("sync", mskg[:], dr["mglob"], f"{tag}_cl")
    p.dma("sync", hwq[:], dr["hw_cols"], f"{tag}_cl")
    p.dma("sync", bqc[:], dr["bq_cols"], f"{tag}_cl")
    p.dma("sync", bkc[:], dr["bk_cols"], f"{tag}_cl")
    p.dma("sync", rotS[:], dr["rotS"], f"{tag}_cl")
    p.dma("sync", ones1[:], dr["ones1"], f"{tag}_cl")
    p.dma("gpsimd", bvr[:], dr["bv_row"], f"{tag}_clg")
    p.dma("gpsimd", bor[:], dr["bo_row"], f"{tag}_clg")
    p.dma("gpsimd", projw[:].rearrange("p (ch n) -> p ch n", n=D),
          dr["Wo"].rearrange("(ch p) n -> p ch n", p=128), f"{tag}_clg")
    p.wait("vector", f"{tag}_cl")
    p.op("vector", lambda e: e.tensor_scalar(hwq[:], hwq[:], 0.125, None, ALU.mult), f"{tag}_hw")

    with contextlib.ExitStack() as pstack:
        c["xlnT"] = pstack.enter_context(nc.sbuf_tensor(f"{tag}_xlnT", [128, DC * T_FULL], BF16))
        attw = pstack.enter_context(nc.sbuf_tensor(f"{tag}_attw", [128, DC * 1536], BF16))
        cosT = pstack.enter_context(nc.sbuf_tensor(f"{tag}_cosT", [128, T_FULL], BF16))
        sinT = pstack.enter_context(nc.sbuf_tensor(f"{tag}_sinT", [128, T_FULL], BF16))
        rr1 = pstack.enter_context(nc.sbuf_tensor(f"{tag}_rr1", [128, 4 * 512], BF16))
        rr2 = pstack.enter_context(nc.sbuf_tensor(f"{tag}_rr2", [128, 4 * 512], BF16))
        rr3 = pstack.enter_context(nc.sbuf_tensor(f"{tag}_rr3", [128, 4 * 512], BF16))
        vrot = pstack.enter_context(nc.sbuf_tensor(f"{tag}_vrot", [128, 4 * KV * VS], BF16))
        ps_q = pstack.enter_context(nc.psum_tensor(f"{tag}_psQ", [128, 512], F32))
        ps_r = pstack.enter_context(nc.psum_tensor(f"{tag}_psR", [128, 512], F32))
        ps_v = pstack.enter_context(nc.psum_tensor(f"{tag}_psV", [128, 256], F32))
        emit_outer_ln(p, c, dr["x_in"], dr["scr"], "ln")
        p.dma("gpsimd", cosT[:], dr["cosT"], f"{tag}_wl")
        p.dma("gpsimd", sinT[:], dr["sinT"], f"{tag}_wl")
        p.dma("gpsimd", attw[:].rearrange("p (ch n) -> p ch n", n=1536)[:, :, 0:1024],
              dr["Wq"].rearrange("(ch p) n -> p ch n", p=128), f"{tag}_wl")
        p.dma("gpsimd", attw[:].rearrange("p (ch n) -> p ch n", n=1536)[:, :, 1024:1280],
              dr["Wk"].rearrange("(ch p) n -> p ch n", p=128), f"{tag}_wl")
        p.dma("gpsimd", attw[:].rearrange("p (ch n) -> p ch n", n=1536)[:, :, 1280:1536],
              dr["Wv"].rearrange("(ch p) n -> p ch n", p=128), f"{tag}_wl")
        p.wait("tensor", f"{tag}_wl")
        p.wait("tensor", f"{tag}_cl")
        p.wait("tensor", f"{tag}_clg")
        p.wait("tensor", f"{tag}_hw")
        p.wait("tensor", "ln_tp")

        def aw(ch, lo, hi):
            return attw[:, ch * 1536 + lo: ch * 1536 + hi]

        s_ev = f"{tag}_pev"
        s_ro = f"{tag}_pro"
        p.wait("vector", f"{tag}_hw")
        for oc in range(10):
            isq = oc < 8
            col0 = oc * 128 if isq else 1024 + (oc - 8) * 128
            for tg in range(8):
                idx = oc * 8 + tg
                rr = idx % 4
                sl = slice(rr * 512, (rr + 1) * 512)
                s_qst = f"{tag}_qst{rr}"
                if idx >= 1:
                    p.wait("tensor", s_ev, 2 * idx - 1)
                for ch in range(DC):
                    p.op("tensor", lambda e, ch=ch, col0=col0, tg=tg: e.matmul(
                        ps_q[:], aw(ch, col0, col0 + 128),
                        xlnT_chunk(c, ch)[:, tg * 512:(tg + 1) * 512],
                        start=(ch == 0), stop=(ch == DC - 1)),
                        f"{tag}_pmm" if ch == DC - 1 else None)
                p.wait("vector", f"{tag}_pmm", p.cnt[f"{tag}_pmm"])
                if idx >= 4:
                    p.wait("vector", s_qst, p.cnt[s_qst])
                bc = bqc[:, oc:oc + 1] if isq else bkc[:, oc - 8:oc - 7]
                if isq:
                    p.op("vector", lambda e, sl=sl, bc=bc, oc=oc: e.tensor_scalar(
                        rr1[:, sl], ps_q[:], bc, hwq[:, oc:oc + 1], ALU.add, ALU.mult), s_ev)
                else:
                    p.op("vector", lambda e, sl=sl, bc=bc: e.tensor_scalar(
                        rr1[:, sl], ps_q[:], bc, None, ALU.add), s_ev)
                p.wait("tensor", s_ev, p.cnt[s_ev])
                if idx >= 1:
                    p.wait("tensor", s_ro, 3 * idx)
                p.op("tensor", lambda e, sl=sl: e.matmul(
                    ps_r[:], rotS[:], rr1[:, sl], start=True, stop=True), s_ro)
                p.wait("vector", s_ro, p.cnt[s_ro])
                tsl = slice(tg * 512, (tg + 1) * 512)
                p.op("vector", lambda e, sl=sl, tsl=tsl: e.tensor_tensor(
                    rr2[:, sl], rr1[:, sl], cosT[:, tsl], ALU.mult), s_ro)
                p.op("vector", lambda e, sl=sl, tsl=tsl: e.tensor_tensor(
                    rr3[:, sl], ps_r[:], sinT[:, tsl], ALU.mult), s_ro)
                p.op("vector", lambda e, sl=sl: e.tensor_tensor(
                    rr1[:, sl], rr2[:, sl], rr3[:, sl], ALU.add), s_ev)
                if isq:
                    p.wait("sync", s_ev, p.cnt[s_ev])
                    qTdv = dr["qTd"].rearrange("(ch pp) t -> ch pp t", pp=128)
                    for hh in (2 * oc, 2 * oc + 1):
                        kv = hh // 4
                        cq = (kv // 2) * 4 + (hh % 4)
                        qrow = 64 * (kv % 2)
                        srow = 64 * (hh % 2)
                        p.dma("sync", qTdv[cq, qrow:qrow + 64, tg * 512:(tg + 1) * 512],
                              rr1[srow:srow + 64, sl], s_qst)
                else:
                    p.op("vector", lambda e, sl=sl, oc=oc, tg=tg: e.tensor_copy(
                        kT[:, (oc - 8) * T_FULL + tg * 512:(oc - 8) * T_FULL + (tg + 1) * 512],
                        rr1[:, sl]), s_qst, n=16)
        s_vev = f"{tag}_vev"
        for t in range(NT):
            if t >= 1:
                p.wait("tensor", s_vev, 3 * t - 2)
            for ch in range(DC):
                p.op("tensor", lambda e, ch=ch, t=t: e.matmul(
                    ps_v[:], xlnT_tile(c, ch, t), aw(ch, 1280, 1536),
                    start=(ch == 0), stop=False), None)
            p.op("tensor", lambda e: e.matmul(
                ps_v[:], ones1[:], bvr[:], start=False, stop=True), f"{tag}_vmm")
            p.wait("vector", f"{tag}_vmm", p.cnt[f"{tag}_vmm"])
            p.op("vector", lambda e, t=t: e.tensor_copy(
                vsb[:, t * KV * VS:(t + 1) * KV * VS].rearrange("p (h w) -> p h w", w=VS)[:, :, 0:64],
                ps_v[:].rearrange("p (h w) -> p h w", w=64)), s_vev)
            p.op("vector", lambda e, t=t: e.memset(
                vsb[:, t * KV * VS:(t + 1) * KV * VS].rearrange("p (h w) -> p h w", w=VS)[:, :, 65:66], 0.0), s_vev)
            p.op("vector", lambda e, t=t: e.memset(
                vsb[:, t * KV * VS:(t + 1) * KV * VS].rearrange("p (h w) -> p h w", w=VS)[:, :, 64:65], 1.0), s_vev)
        p.wait("vector", s_ev)
        p.wait("vector", s_vev)
        for r in range(4):
            p.wait("vector", f"{tag}_qst{r}")
        p.op("vector", lambda e: e.engine_nop(), f"{tag}_Pdone")
    for r in range(4):
        p.wait("sync", f"{tag}_qst{r}")

    qblk = stack.enter_context(nc.sbuf_tensor(f"{tag}_qblk", [128, 2 * DC * 512], BF16))
    Pt = stack.enter_context(nc.sbuf_tensor(f"{tag}_Pt", [128, 2 * 512], BF16))
    Ptf = stack.enter_context(nc.sbuf_tensor(f"{tag}_Ptf", [128, 256], BF16))
    Osb = stack.enter_context(nc.sbuf_tensor(f"{tag}_Osb", [128, 2 * 4 * D], BF16))
    OTsb = stack.enter_context(nc.sbuf_tensor(f"{tag}_OTsb", [128, 2 * DC * 512], BF16))
    zr = stack.enter_context(nc.sbuf_tensor(f"{tag}_zr", [128, 4], F32))
    ps_s = stack.enter_context(nc.psum_tensor(f"{tag}_psS", [128, 2 * 512], F32))
    ps_o = stack.enter_context(nc.psum_tensor(f"{tag}_psO", [128, 264], F32))
    ps_f = stack.enter_context(nc.psum_tensor(f"{tag}_psF", [128, 256], F32))
    ps_pr = stack.enter_context(nc.psum_tensor(f"{tag}_psPr", [128, 1024], F32))
    for eng in ("gpsimd", "tensor", "vector", "scalar", "sync"):
        p.wait(eng, f"{tag}_Pdone")

    qTd = dr["qTd"].rearrange("(ch pp) t -> ch pp t", pp=128)
    s_hd = f"{tag}_hd"

    def kchunks(t):
        base = min(max(t - 1, 0), 29)
        var = 0 if t == 0 else (1 if t == 1 else (2 if t <= 30 else 3))
        return base, var

    for g in range(8):
        rb = g % 2
        s_qb = f"{tag}_qb{rb}"
        qb = qblk[:, rb * DC * 512:(rb + 1) * DC * 512]
        if g >= 2:
            p.wait("gpsimd", s_hd, 3 * 64 * (g - 1))
        for ch in range(DC):
            p.dma("gpsimd", qb[:, ch * 512:(ch + 1) * 512], qTd[ch, :, g * 512:(g + 1) * 512], s_qb)
        p.wait("tensor", s_qb, p.cnt[s_qb])
        if g >= 2:
            p.wait("vector", f"{tag}_ost{rb}", p.cnt[f"{tag}_ost{rb}"])
        for tt in range(4):
            t = g * 4 + tt
            base, var = kchunks(t)
            for h in range(H):
                hv = h // 4
                hcc = hv // 2
                hrr = 64 * (hv % 2)
                ti = (t * H + h)
                rp = ti % 2
                pss = ps_s[:, rp * 512:(rp + 1) * 512]
                pso = ps_o[:, rp * 66:(rp + 1) * 66]
                pt = Pt[:, rp * 512:(rp + 1) * 512]
                if ti >= 2:
                    p.wait("tensor", s_hd, 3 * (ti - 1) + 2)
                cq = (hv // 2) * 4 + (h % 4)
                qap = qb[hrr:hrr + 64, cq * 512 + tt * 128: cq * 512 + (tt + 1) * 128]
                for cc in range(3):
                    p.op("tensor", lambda e, cc=cc, base=base, hrr=hrr, hcc=hcc, qap=qap, pss=pss: e.matmul(
                        pss[:, cc * 128:(cc + 1) * 128],
                        kT[hrr:hrr + 64, hcc * T_FULL + (base + cc) * 128: hcc * T_FULL + (base + cc + 1) * 128],
                        qap, start=True, stop=True), None)
                p.op("tensor", lambda e, hrr=hrr, hcc=hcc, qap=qap, pss=pss: e.matmul(
                    pss[:, 384:512],
                    kT[hrr:hrr + 64, hcc * T_FULL: hcc * T_FULL + 128],
                    qap, start=True, stop=True), f"{tag}_sc")
                p.wait("vector", f"{tag}_sc", p.cnt[f"{tag}_sc"])
                if ti >= 2:
                    p.wait("vector", s_hd, 3 * (ti - 2) + 2)
                p.op("vector", lambda e, pss=pss, var=var: e.tensor_tensor(
                    pss[:, 0:384], pss[:, 0:384], mskl[:, var * 384:(var + 1) * 384], ALU.add), None)
                p.op("vector", lambda e, pss=pss: e.tensor_tensor(
                    pss[:, 384:512], pss[:, 384:512], mskg[:], ALU.add), s_hd)
                p.wait("scalar", s_hd, p.cnt[s_hd])
                if ti >= 2:
                    p.wait("scalar", f"{tag}_pv", ti - 1)
                p.op("scalar", lambda e, pt=pt, pss=pss: e.activation(pt, pss, AF.Exp), s_hd)
                p.wait("tensor", s_hd, p.cnt[s_hd])
                if ti >= 2:
                    p.wait("tensor", s_hd, 3 * (ti - 1) + 3)
                for cc in range(3):
                    p.op("tensor", lambda e, cc=cc, base=base, hv=hv, pt=pt, pso=pso: e.matmul(
                        pso, pt[:, cc * 128:(cc + 1) * 128],
                        vsb[:, (base + cc) * KV * VS + hv * VS: (base + cc) * KV * VS + (hv + 1) * VS],
                        start=(cc == 0), stop=False), None)
                p.op("tensor", lambda e, hv=hv, pt=pt, pso=pso: e.matmul(
                    pso, pt[:, 384:512],
                    vsb[:, 0 * KV * VS + hv * VS: 0 * KV * VS + (hv + 1) * VS],
                    start=False, stop=True), f"{tag}_pv")
                p.wait("vector", f"{tag}_pv", p.cnt[f"{tag}_pv"])
                p.op("vector", lambda e, pso=pso, rp=rp: e.reciprocal(
                    zr[:, rp:rp + 1], pso[:, 64:65]), f"{tag}_zz")
                p.wait("vector", f"{tag}_zz", p.cnt[f"{tag}_zz"])
                p.op("vector", lambda e, pso=pso, rp=rp, rb=rb, tt=tt, h=h: e.tensor_scalar(
                    Osb[:, rb * 4 * D + tt * D + h * 64: rb * 4 * D + tt * D + (h + 1) * 64],
                    pso[:, 0:64], zr[:, rp:rp + 1], None, ALU.mult), s_hd)
            if t == 0:
                p.wait("tensor", s_hd, p.cnt[s_hd])
                for h in range(H):
                    hv = h // 4
                    hcc = hv // 2
                    hrr = 64 * (hv % 2)
                    cq = (hv // 2) * 4 + (h % 4)
                    qap = qb[hrr:hrr + 64, cq * 512: cq * 512 + 8]
                    if h >= 1:
                        p.wait("tensor", f"{tag}_fex", h)
                    for ck in range(NT):
                        p.op("tensor", lambda e, ck=ck, hrr=hrr, hcc=hcc, qap=qap: e.matmul(
                            ps_f[:, ck * 8:(ck + 1) * 8],
                            kT[hrr:hrr + 64, hcc * T_FULL + ck * 128: hcc * T_FULL + (ck + 1) * 128],
                            qap, start=True, stop=True),
                            f"{tag}_fsc" if ck == NT - 1 else None)
                    p.wait("scalar", f"{tag}_fsc", p.cnt[f"{tag}_fsc"])
                    if h >= 1:
                        p.wait("scalar", f"{tag}_fpv", h)
                    p.op("scalar", lambda e: e.activation(Ptf[:], ps_f[:], AF.Exp), f"{tag}_fex")
                    p.wait("tensor", f"{tag}_fex", p.cnt[f"{tag}_fex"])
                    if h >= 1:
                        p.wait("tensor", f"{tag}_f7", h)
                    for ck in range(NT):
                        p.op("tensor", lambda e, ck=ck, hv=hv: e.matmul(
                            ps_o[0:8, 132:198],
                            Ptf[:, ck * 8:(ck + 1) * 8],
                            vsb[:, ck * KV * VS + hv * VS: ck * KV * VS + (hv + 1) * VS],
                            start=(ck == 0), stop=(ck == NT - 1)),
                            f"{tag}_fpv" if ck == NT - 1 else None)
                    p.wait("vector", f"{tag}_fpv", p.cnt[f"{tag}_fpv"])
                    p.op("vector", lambda e: e.reciprocal(zr[0:8, 2:3], ps_o[0:8, 196:197]), f"{tag}_fzz")
                    p.wait("vector", f"{tag}_fzz", p.cnt[f"{tag}_fzz"])
                    p.op("vector", lambda e, h=h: e.tensor_scalar(
                        Osb[0:8, h * 64:(h + 1) * 64],
                        ps_o[0:8, 132:196], zr[0:8, 2:3], None, ALU.mult), f"{tag}_f7")
                p.wait("tensor", f"{tag}_f7", p.cnt[f"{tag}_f7"])
        s_ost = f"{tag}_ost{rb}"
        p.wait("sync", s_hd)
        if g == 0:
            p.wait("sync", f"{tag}_f7", p.cnt[f"{tag}_f7"])
        if g >= 2:
            p.wait("sync", s_ost, p.cnt[s_ost])
        for j in range(4):
            p.dma("sync", dr["Od"].rearrange("(t pp) d -> t pp d", pp=128)[g * 4 + j],
                  Osb[:, rb * 4 * D + j * D:(rb * 4 * D) + (j + 1) * D], s_ost)
        p.wait("sync", s_ost, p.cnt[s_ost])
        s_ot = f"{tag}_ot{rb}"
        if g >= 2:
            p.wait("sync", f"{tag}_prmm", 4 * (g - 1))
        for ch in range(DC):
            p.dmat("sync", ot_view(OTsb, rb, ch),
                   dr["Od"][g * 512:(g + 1) * 512, ch * 128:(ch + 1) * 128], s_ot)
        p.wait("tensor", s_ot, p.cnt[s_ot])
        for s in range(4):
            gt = g * 4 + s
            if gt >= 1:
                p.wait("tensor", f"{tag}_rev", 2 * gt)
            for half in range(2):
                for ch in range(DC):
                    p.op("tensor", lambda e, s=s, half=half, ch=ch, rb=rb: e.matmul(
                        ps_pr[:, half * 512:(half + 1) * 512],
                        ot_view(OTsb, rb, ch)[:, s * 128:(s + 1) * 128],
                        projw[:, ch * D + half * 512: ch * D + (half + 1) * 512],
                        start=(ch == 0), stop=False), None)
                p.op("tensor", lambda e, half=half: e.matmul(
                    ps_pr[:, half * 512:(half + 1) * 512], ones1[:],
                    bor[:, half * 512:(half + 1) * 512], start=False, stop=True),
                    f"{tag}_prmm" if half == 1 else None)
            p.wait("vector", f"{tag}_prmm", p.cnt[f"{tag}_prmm"])
            emit_residual_tile(p, c, dr["x_in"], dr["x_out"], gt,
                               [ps_pr[:, 0:512], ps_pr[:, 512:1024]], tag,
                               f"{tag}_prmm", p.cnt[f"{tag}_prmm"],
                               guard=("ln_app", NT))
    finish_store(p, tag)


def build_l0_attn_b():
    nc = bass.Bass("TRN2", target_bir_lowering=False)
    x_in = nc.dram_tensor("x_in", [T_FULL, D], F32, kind="ExternalInput").ap()
    Wq = nc.dram_tensor("Wq", [D, D], F32, kind="ExternalInput").ap()
    Wk = nc.dram_tensor("Wk", [D, 256], F32, kind="ExternalInput").ap()
    Wv = nc.dram_tensor("Wv", [D, 256], F32, kind="ExternalInput").ap()
    Wo = nc.dram_tensor("Wo", [D, D], F32, kind="ExternalInput").ap()
    bq_cols = nc.dram_tensor("bq_cols", [128, 8], F32, kind="ExternalInput").ap()
    bk_cols = nc.dram_tensor("bk_cols", [128, 2], F32, kind="ExternalInput").ap()
    bv_row = nc.dram_tensor("bv_row", [1, 256], F32, kind="ExternalInput").ap()
    bo_row = nc.dram_tensor("bo_row", [1, D], F32, kind="ExternalInput").ap()
    hw_cols = nc.dram_tensor("hw_cols", [128, 8], F32, kind="ExternalInput").ap()
    cosT = nc.dram_tensor("cosT", [128, T_FULL], BF16, kind="ExternalInput").ap()
    sinT = nc.dram_tensor("sinT", [128, T_FULL], BF16, kind="ExternalInput").ap()
    rotS = nc.dram_tensor("rotS", [128, 128], BF16, kind="ExternalInput").ap()
    mloc = nc.dram_tensor("mloc", [128, 4 * 384], F32, kind="ExternalInput").ap()
    mglob = nc.dram_tensor("mglob", [128, 128], F32, kind="ExternalInput").ap()
    ones1 = nc.dram_tensor("ones1", [1, 128], BF16, kind="ExternalInput").ap()
    x_out = nc.dram_tensor("x_out", [T_FULL, D], F32, kind="ExternalOutput").ap()
    scr = nc.dram_tensor("scr", [T_FULL, D], BF16, kind="Internal").ap()
    qTd = nc.dram_tensor("qTd", [D, T_FULL], BF16, kind="Internal").ap()
    Od = nc.dram_tensor("Od", [T_FULL, D], BF16, kind="Internal").ap()
    with contextlib.ExitStack() as stack:
        p = Prog(nc, stack)
        c = alloc_ln(nc, stack, with_xlnT=False)
        dr = dict(Wq=Wq, Wk=Wk, Wv=Wv, Wo=Wo, bq_cols=bq_cols, bk_cols=bk_cols,
                  bv_row=bv_row, bo_row=bo_row, hw_cols=hw_cols, cosT=cosT,
                  sinT=sinT, rotS=rotS, mloc=mloc, mglob=mglob, ones1=ones1,
                  x_in=x_in, x_out=x_out, scr=scr, qTd=qTd, Od=Od)
        emit_l0_attn_b(p, nc, c, stack, dr)
        p.check_deadlock()
        p.realize()
    return nc



def emit_l1_ffn(p, nc, c, stack, dr, tag="f1", n_cores=8):
    NP = 2048
    s_cl = f"{tag}_cl"
    s_clg = f"{tag}_clg"

    Yt = stack.enter_context(nc.sbuf_tensor(f"{tag}_Yt", [128, 16 * D], BF16))
    phib = stack.enter_context(nc.sbuf_tensor(f"{tag}_phib", [1, NP], BF16))
    ones1 = stack.enter_context(nc.sbuf_tensor(f"{tag}_ones1", [1, 128], BF16))
    expb = stack.enter_context(nc.sbuf_tensor(f"{tag}_expb", [64, D], BF16))
    ident = stack.enter_context(nc.sbuf_tensor(f"{tag}_ident", [128, 128], F32))

    p.dma("gpsimd", phib[:], dr["phi_b_row"], s_clg)
    p.dma("gpsimd", expb[:], dr["expert_b"], s_clg)
    p.dma("sync", ones1[:], dr["ones1"], s_cl)
    p.dma("sync", ident[:], dr["ident"], s_cl)

    Edv = dr["Ed"].rearrange("(t p) e -> t p e", p=128)
    Xdv = dr["Xd"].rearrange("(s p) d -> s p d", p=128)
    Xrv = dr["Xr"].rearrange("(s p) d -> s p d", p=128)

    with contextlib.ExitStack() as ps0:
        c["xlnT"] = ps0.enter_context(nc.sbuf_tensor(f"{tag}_xlnT", [128, DC * T_FULL], BF16))
        phiT = ps0.enter_context(nc.sbuf_tensor(f"{tag}_phiT", [128, DC * NP], BF16))
        with nc.sbuf_tensor(f"{tag}_phld", [128, 16 * D], BF16) as phld:
            p.dma("gpsimd", phld[:].rearrange("p (ch n) -> p ch n", n=D),
                  dr["phi_w"].rearrange("(ch p) n -> p ch n", p=128), f"{tag}_pl")
            p.wait("sync", f"{tag}_pl")
            p.dma("sync", dr["scrP"].rearrange("(ch p) n -> p ch n", p=128),
                  phld[:].rearrange("p (ch n) -> p ch n", n=D), f"{tag}_ps")
            p.wait("sync", f"{tag}_ps", p.cnt[f"{tag}_ps"])
        ebuf = ps0.enter_context(nc.sbuf_tensor(f"{tag}_ebuf", [128, NT * 256], BF16))
        sD = ps0.enter_context(nc.sbuf_tensor(f"{tag}_sD", [128, NT * 8], F32))
        xrot2 = ps0.enter_context(nc.sbuf_tensor(f"{tag}_xrot2", [128, 4 * D], BF16))
        dsub = ps0.enter_context(nc.sbuf_tensor(f"{tag}_dsub", [128, 4 * 128], BF16))
        xev = ps0.enter_context(nc.sbuf_tensor(f"{tag}_xev", [128, 2 * D], F32))
        ps_l = ps0.enter_context(nc.psum_tensor(f"{tag}_psL", [128, 2 * 512], F32))
        ps_x = ps0.enter_context(nc.psum_tensor(f"{tag}_psX", [128, 2 * D], F32))

        emit_outer_ln(p, c, dr["x_in"], dr["scr"], "ln")
        p.wait("scalar", f"{tag}_ps")
        for ch in range(DC):
            p.dmat("sync", phiT[:, ch * NP:(ch + 1) * NP],
                   dr["scrP"][:, ch * 128:(ch + 1) * 128], f"{tag}_pt")
        p.wait("tensor", f"{tag}_pt")
        p.wait("tensor", s_cl)
        p.wait("tensor", s_clg)
        p.wait("tensor", "ln_tp")
        xlsrc = dr["scr"].rearrange("(t p) d -> t p d", p=128)

        for n8 in range(8):
            col0 = n8 * 256
            for t in range(NT):
                li = n8 * NT + t
                rp = li % 2
                psl = ps_l[:, rp * 512:rp * 512 + 256]
                if li >= 2:
                    p.wait("tensor", f"{tag}_ex", li - 1)
                for ch in range(DC):
                    p.op("tensor", lambda e, ch=ch, t=t, col0=col0, psl=psl: e.matmul(
                        psl, xlnT_tile(c, ch, t),
                        phiT[:, ch * NP + col0: ch * NP + col0 + 256],
                        start=(ch == 0), stop=False), None)
                p.op("tensor", lambda e, col0=col0, psl=psl: e.matmul(
                    psl, ones1[:], phib[:, col0:col0 + 256],
                    start=False, stop=True), f"{tag}_lm")
                p.wait("scalar", f"{tag}_lm", p.cnt[f"{tag}_lm"])
                if li >= NT:
                    p.wait("scalar", f"{tag}_ds", 2 * NT * (li // NT))
                    p.wait("scalar", f"{tag}_est", 16 * (li // NT))
                p.op("scalar", lambda e, t=t, psl=psl: e.activation(
                    ebuf[:, t * 256:(t + 1) * 256], psl, AF.Exp), f"{tag}_ex")
                p.wait("vector", f"{tag}_ex", p.cnt[f"{tag}_ex"])
                p.op("vector", lambda e, t=t: e.tensor_reduce(
                    sD[:, t * 8:(t + 1) * 8],
                    ebuf[:, t * 256:(t + 1) * 256].rearrange("p (n q) -> p n q", q=32),
                    AX.X, ALU.add), f"{tag}_sd")
            p.wait("sync", f"{tag}_ex")
            p.dma("sync", dr["Ed"].rearrange("(t p) e -> p t e", p=128)[:, :, col0:col0 + 256],
                  ebuf[:].rearrange("p (t e) -> p t e", e=256), f"{tag}_est")
            p.wait("vector", f"{tag}_sd")
            p.op("vector", lambda e: e.reciprocal(sD[:], sD[:]), f"{tag}_rd")
            p.wait("vector", f"{tag}_rd", p.cnt[f"{tag}_rd"])
            if n8 >= 1:
                p.wait("tensor", f"{tag}_dmm", 2 * n8)
            for t in range(NT):
                r = t % 4
                xt2 = xrot2[:, r * D:(r + 1) * D]
                s_x2 = f"{tag}_x2{r}"
                if t >= 4:
                    p.wait("sync", f"{tag}_xp", NT * n8 + (t - 4) + 1)
                vld = p.dma("sync", xt2, xlsrc[t], s_x2)
                for npsub in range(2):
                    dsl = (t % 2) * 2 + npsub
                    ds = dsub[:, dsl * 128:(dsl + 1) * 128]
                    if t >= 2:
                        p.wait("vector", f"{tag}_xp", NT * n8 + (t - 2) + 1)
                    p.op("vector", lambda e, t=t, npsub=npsub, ds=ds: e.tensor_tensor(
                        ds.rearrange("p (n q) -> p n q", q=32),
                        ebuf[:, t * 256 + npsub * 128: t * 256 + npsub * 128 + 128].rearrange("p (n q) -> p n q", q=32),
                        sD[:, t * 8 + npsub * 4: t * 8 + npsub * 4 + 4].rearrange("p (n o) -> p n o", o=1).broadcast_to([128, 4, 32]),
                        ALU.mult), f"{tag}_ds")
                    p.wait("tensor", f"{tag}_ds", p.cnt[f"{tag}_ds"])
                    if npsub == 0:
                        p.wait("tensor", s_x2, p.cnt[s_x2])
                    for half in range(2):
                        p.op("tensor", lambda e, ds=ds, xt2=xt2, npsub=npsub, half=half, t=t: e.matmul(
                            ps_x[:, npsub * D + half * 512: npsub * D + half * 512 + 512],
                            ds, xt2[:, half * 512:(half + 1) * 512],
                            start=(t == 0), stop=(t == NT - 1)),
                            f"{tag}_xp" if (npsub == 1 and half == 1) else None)
            p.wait("vector", f"{tag}_xp", p.cnt[f"{tag}_xp"])
            if n8 >= 1:
                p.wait("vector", f"{tag}_xst", p.cnt[f"{tag}_xst"])
            for npsub in range(2):
                p.op("vector", lambda e, npsub=npsub: e.tensor_copy(
                    xev[:, npsub * D:(npsub + 1) * D], ps_x[:, npsub * D:(npsub + 1) * D]),
                    f"{tag}_dmm")
            p.wait("sync", f"{tag}_dmm", p.cnt[f"{tag}_dmm"])
            for npsub in range(2):
                p.dma("sync", Xdv[n8 * 2 + npsub], xev[:, npsub * D:(npsub + 1) * D], f"{tag}_xst")
        p.wait("gpsimd", f"{tag}_xst")
        p.wait("gpsimd", f"{tag}_est")
        scc = p.sem(f"{tag}_cc")
        vcc = p.inc(f"{tag}_cc", 1)
        p.emit("gpsimd", lambda e: e.collective_compute(
            "AllReduce", ALU.add,
            replica_groups=[list(range(n_cores))],
            ins=[dr["Xd"]], outs=[dr["Xr"]]).then_inc(scc, 1),
            ("inc", f"{tag}_cc", 1))

    with contextlib.ExitStack() as ps1:
        XnT = ps1.enter_context(nc.sbuf_tensor(f"{tag}_XnT", [128, DC * NP], BF16))
        xbr = ps1.enter_context(nc.sbuf_tensor(f"{tag}_xbr", [128, 2 * D], F32))
        xbs = ps1.enter_context(nc.sbuf_tensor(f"{tag}_xbs", [128, 2 * D], BF16))
        bn2 = ps1.enter_context(nc.sbuf_tensor(f"{tag}_bn2", [128, 16], F32))
        expw = ps1.enter_context(nc.sbuf_tensor(f"{tag}_expw", [128, 2 * DC * D], BF16))
        yst = ps1.enter_context(nc.sbuf_tensor(f"{tag}_yst", [32, 2 * D], BF16))
        ps_y = ps1.enter_context(nc.psum_tensor(f"{tag}_psY", [32, 2 * D], F32))
        p.wait("sync", f"{tag}_cc")
        for sc in range(16):
            r = sc % 2
            xc = xbr[:, r * D:(r + 1) * D]
            xs = xbs[:, r * D:(r + 1) * D]
            s_xl = f"{tag}_xl{r}"
            s_xs = f"{tag}_xs{r}"
            if sc >= 2:
                p.wait("sync", f"{tag}_th", sc - 1)
            vld = p.dma("sync", xc, Xrv[sc], s_xl)
            p.wait("vector", s_xl, vld)
            p.op("vector", lambda e, xc=xc: e.tensor_scalar(
                xc, xc, 65000.0, -33000.0, ALU.min, ALU.max), f"{tag}_clip")
            p.wait("vector", f"{tag}_clip", p.cnt[f"{tag}_clip"])
            p.op("vector", lambda e, xc=xc, sc=sc: e.bn_stats(bn2[:, 0:6], xc[:, 0:512]))
            p.op("vector", lambda e, xc=xc, sc=sc: e.bn_stats(bn2[:, 6:12], xc[:, 512:1024]), f"{tag}_bnp")
            p.wait("vector", f"{tag}_bnp", p.cnt[f"{tag}_bnp"])
            p.op("vector", lambda e: e.bn_aggr(bn2[:, 12:14], bn2[:, 0:12]), f"{tag}_bnp")
            p.wait("vector", f"{tag}_bnp", p.cnt[f"{tag}_bnp"])
            p.op("vector", lambda e: e.tensor_scalar(bn2[:, 13:14], bn2[:, 13:14], 1e-5, None, ALU.add), f"{tag}_bnp")
            p.wait("scalar", f"{tag}_bnp", p.cnt[f"{tag}_bnp"])
            p.op("scalar", lambda e: e.activation(bn2[:, 13:14], bn2[:, 13:14], AF.Ln), f"{tag}_rs")
            p.wait("scalar", f"{tag}_rs", p.cnt[f"{tag}_rs"])
            p.op("scalar", lambda e: e.activation(bn2[:, 13:14], bn2[:, 13:14], AF.Exp, scale=-0.5), f"{tag}_rs")
            p.wait("vector", f"{tag}_rs", p.cnt[f"{tag}_rs"])
            p.op("vector", lambda e, xc=xc: e.tensor_scalar(
                xc, xc, bn2[:, 12:13], bn2[:, 13:14], ALU.subtract, ALU.mult), f"{tag}_app2")
            p.wait("scalar", f"{tag}_app2", p.cnt[f"{tag}_app2"])
            if sc >= 2:
                p.wait("scalar", s_xs, p.cnt[s_xs])
            p.op("scalar", lambda e, xc=xc, xs=xs: e.activation(xs, xc, AF.Tanh), f"{tag}_th")
            p.wait("sync", f"{tag}_th", p.cnt[f"{tag}_th"])
            p.dma("sync", dr["scrX"].rearrange("(s p) d -> s p d", p=128)[sc], xs, s_xs)
        for r in range(2):
            p.wait("sync", f"{tag}_xs{r}")
        for ch in range(DC):
            p.dmat("sync", XnT[:, ch * NP:(ch + 1) * NP],
                   dr["scrX"][:, ch * 128:(ch + 1) * 128], f"{tag}_xt")
        p.wait("tensor", f"{tag}_xt")
        for n in range(64):
            r = n % 2
            s_ew = f"{tag}_ew{r}"
            ew = expw[:, r * DC * D:(r + 1) * DC * D]
            if n >= 2:
                p.wait("gpsimd", f"{tag}_ymm", n - 1)
            p.dma("gpsimd", ew.rearrange("p (ch n) -> p ch n", n=D),
                  dr["expert_w"][n].rearrange("(ch p) n -> p ch n", p=128), s_ew)
            p.wait("tensor", s_ew, p.cnt[s_ew])
            if n >= 2:
                p.wait("tensor", f"{tag}_yev", n - 1)
            for half in range(2):
                for ch in range(DC):
                    p.op("tensor", lambda e, ch=ch, n=n, half=half, ew=ew: e.matmul(
                        ps_y[:, (n % 2) * D + half * 512:(n % 2) * D + half * 512 + 512],
                        XnT[:, ch * NP + n * 32: ch * NP + (n + 1) * 32],
                        ew[:, ch * D + half * 512: ch * D + (half + 1) * 512],
                        start=(ch == 0), stop=(ch == DC - 1)),
                        f"{tag}_ymm" if (half == 1 and ch == DC - 1) else None)
            p.wait("vector", f"{tag}_ymm", p.cnt[f"{tag}_ymm"])
            if n >= 2:
                p.wait("vector", f"{tag}_yst{n % 2}", p.cnt[f"{tag}_yst{n % 2}"])
            p.op("vector", lambda e, n=n: e.tensor_copy(
                yst[:, (n % 2) * D:((n % 2) + 1) * D],
                ps_y[:, (n % 2) * D:((n % 2) + 1) * D]), f"{tag}_yev")
            p.wait("sync", f"{tag}_yev", p.cnt[f"{tag}_yev"])
            p.dma("sync", Yt[:, (n // 4) * D:(n // 4 + 1) * D].rearrange(
                "p d -> p d")[32 * (n % 4):32 * (n % 4) + 32, :],
                yst[:, (n % 2) * D:((n % 2) + 1) * D], f"{tag}_yst{n % 2}")
        for r in range(2):
            p.wait("sync", f"{tag}_yst{r}")

    with contextlib.ExitStack() as ps2:
        et = ps2.enter_context(nc.sbuf_tensor(f"{tag}_et", [128, 2 * NP], BF16))
        ct = ps2.enter_context(nc.sbuf_tensor(f"{tag}_ct", [128, 2 * NP], BF16))
        rC = ps2.enter_context(nc.sbuf_tensor(f"{tag}_rC", [128, 2 * 32], F32))
        SCn = ps2.enter_context(nc.sbuf_tensor(f"{tag}_SCn", [128, 2 * 64], F32))
        SCnT = ps2.enter_context(nc.sbuf_tensor(f"{tag}_SCnT", [64, 4 * 128], BF16))
        CT = ps2.enter_context(nc.sbuf_tensor(f"{tag}_CT", [128, 2 * 16 * 512], BF16))
        ps_t = ps2.enter_context(nc.psum_tensor(f"{tag}_psT", [64, 2 * 128], F32))
        ps_pr = ps2.enter_context(nc.psum_tensor(f"{tag}_psPr", [128, 1024], F32))
        for g in range(8):
            rb = g % 2
            ctv = CT[:, rb * 16 * 512:(rb + 1) * 16 * 512]
            for tt in range(4):
                t = g * 4 + tt
                r = t % 2
                ete = et[:, r * NP:(r + 1) * NP]
                cte = ct[:, r * NP:(r + 1) * NP]
                s_el = f"{tag}_el{r}"
                if t >= 2:
                    p.wait("sync", f"{tag}_cst{r}", p.cnt[f"{tag}_cst{r}"])
                vld = p.dma("sync", ete, Edv[t], s_el)
                p.wait("vector", s_el, vld)
                p.op("vector", lambda e, ete=ete, r=r: e.tensor_reduce(
                    rC[:, r * 32:(r + 1) * 32],
                    ete.rearrange("p (n q) -> p q n", q=32), AX.X, ALU.add), f"{tag}_sc")
                p.wait("vector", f"{tag}_sc", p.cnt[f"{tag}_sc"])
                p.op("vector", lambda e, r=r: e.reciprocal(
                    rC[:, r * 32:(r + 1) * 32], rC[:, r * 32:(r + 1) * 32]), f"{tag}_rc")
                p.wait("vector", f"{tag}_rc", p.cnt[f"{tag}_rc"])
                p.op("vector", lambda e, ete=ete, cte=cte, r=r: e.tensor_tensor(
                    cte.rearrange("p (n q) -> p n q", q=32),
                    ete.rearrange("p (n q) -> p n q", q=32),
                    rC[:, r * 32:(r + 1) * 32].rearrange("p (o q) -> p o q", o=1).broadcast_to([128, 64, 32]),
                    ALU.mult), f"{tag}_ct")
                p.wait("vector", f"{tag}_ct", p.cnt[f"{tag}_ct"])
                if t >= 2:
                    p.wait("vector", f"{tag}_tp2", t - 1)
                p.op("vector", lambda e, cte=cte, r=r: e.tensor_reduce(
                    SCn[:, r * 64:(r + 1) * 64],
                    cte.rearrange("p (n q) -> p n q", q=32), AX.X, ALU.add), f"{tag}_scn")
                p.wait("tensor", f"{tag}_scn", p.cnt[f"{tag}_scn"])
                if t >= 2:
                    p.wait("tensor", f"{tag}_tpe", t - 1)
                p.op("tensor", lambda e, r=r: e.transpose(
                    ps_t[:, r * 128:(r + 1) * 128], SCn[:, r * 64:(r + 1) * 64], ident[:]), f"{tag}_tp2")
                p.wait("vector", f"{tag}_tp2", p.cnt[f"{tag}_tp2"])
                if g >= 1:
                    p.wait("vector", f"{tag}_prmm", 4 * (g - 1) + tt + 1)
                p.op("vector", lambda e, tt=tt, r=r: e.tensor_copy(
                    SCnT[:, tt * 128:(tt + 1) * 128], ps_t[:, r * 128:(r + 1) * 128]), f"{tag}_tpe")
                p.wait("sync", f"{tag}_ct", p.cnt[f"{tag}_ct"])
                p.dma("sync", dr["Cd"].rearrange("(t p) e -> t p e", p=128)[t], cte, f"{tag}_cst{r}")
            for r in range(2):
                p.wait("sync", f"{tag}_cst{r}", p.cnt[f"{tag}_cst{r}"])
            if g >= 2:
                p.wait("sync", f"{tag}_prmm", 4 * (g - 1))
            for ch in range(16):
                p.dmat("sync", ctv[:, ch * 512:(ch + 1) * 512],
                       dr["Cd"][g * 512:(g + 1) * 512, ch * 128:(ch + 1) * 128], f"{tag}_ctt{rb}")
            p.wait("tensor", f"{tag}_ctt{rb}", p.cnt[f"{tag}_ctt{rb}"])
            p.wait("tensor", f"{tag}_tpe")
            for tt in range(4):
                gt = g * 4 + tt
                t = gt
                if gt >= 1:
                    p.wait("tensor", f"{tag}_rev", 2 * gt)
                for half in range(2):
                    for ch in range(16):
                        p.op("tensor", lambda e, ch=ch, tt=tt, half=half, ctv=ctv: e.matmul(
                            ps_pr[:, half * 512:(half + 1) * 512],
                            ctv[:, ch * 512 + tt * 128: ch * 512 + (tt + 1) * 128],
                            Yt[:, ch * D + half * 512: ch * D + (half + 1) * 512],
                            start=(ch == 0), stop=False), None)
                    p.op("tensor", lambda e, half=half, tt=tt: e.matmul(
                        ps_pr[:, half * 512:(half + 1) * 512],
                        SCnT[:, tt * 128:(tt + 1) * 128],
                        expb[:, half * 512:(half + 1) * 512],
                        start=False, stop=True),
                        f"{tag}_prmm" if half == 1 else None)
                p.wait("vector", f"{tag}_prmm", p.cnt[f"{tag}_prmm"])
                emit_residual_tile(p, c, dr["x_in"], dr["x_out"], gt,
                                   [ps_pr[:, 0:512], ps_pr[:, 512:1024]], tag,
                                   f"{tag}_prmm", p.cnt[f"{tag}_prmm"],
                                   guard=("ln_app", NT))
    finish_store(p, tag)


def build_l1_ffn(n_cores=8):
    nc = bass.Bass("TRN2", target_bir_lowering=False)
    x_in = nc.dram_tensor("x_in", [T_FULL, D], F32, kind="ExternalInput").ap()
    phi_w = nc.dram_tensor("phi_w", [2048, D], F32, kind="ExternalInput").ap()
    phi_b_row = nc.dram_tensor("phi_b_row", [1, 2048], F32, kind="ExternalInput").ap()
    expert_w = nc.dram_tensor("expert_w", [64, D, D], F32, kind="ExternalInput").ap()
    expert_b = nc.dram_tensor("expert_b", [64, D], F32, kind="ExternalInput").ap()
    ones1 = nc.dram_tensor("ones1", [1, 128], BF16, kind="ExternalInput").ap()
    ident = nc.dram_tensor("ident", [128, 128], F32, kind="ExternalInput").ap()
    x_out = nc.dram_tensor("x_out", [T_FULL, D], F32, kind="ExternalOutput").ap()
    scr = nc.dram_tensor("scr", [T_FULL, D], BF16, kind="Internal").ap()
    scrP = nc.dram_tensor("scrP", [2048, D], BF16, kind="Internal").ap()
    Ed = nc.dram_tensor("Ed", [T_FULL, 2048], BF16, kind="Internal").ap()
    Xd = nc.dram_tensor("Xd", [2048, D], F32, kind="Internal").ap()
    Xr = nc.dram_tensor("Xr", [2048, D], F32, kind="Internal").ap()
    scrX = nc.dram_tensor("scrX", [2048, D], BF16, kind="Internal").ap()
    Cd = nc.dram_tensor("Cd", [T_FULL, 2048], BF16, kind="Internal").ap()
    with contextlib.ExitStack() as stack:
        p = Prog(nc, stack)
        c = alloc_ln(nc, stack, with_xlnT=False)
        dr = dict(phi_w=phi_w, phi_b_row=phi_b_row, expert_w=expert_w,
                  expert_b=expert_b, ones1=ones1, ident=ident, x_in=x_in,
                  x_out=x_out, scr=scr, scrP=scrP, Ed=Ed, Xd=Xd, Xr=Xr,
                  scrX=scrX, Cd=Cd)
        emit_l1_ffn(p, nc, c, stack, dr, n_cores=n_cores)
        p.check_deadlock()
        p.realize()
    return nc



def emit_l1_attn_b(p, nc, c, stack, dr, tag="d1"):
    H = 4
    DH = 256

    beta = stack.enter_context(nc.sbuf_tensor(f"{tag}_beta", [128, NT * 4], F32))
    nbeta = stack.enter_context(nc.sbuf_tensor(f"{tag}_nbeta", [128, NT * 4], F32))
    Lm = stack.enter_context(nc.sbuf_tensor(f"{tag}_Lm", [128, 128], F32))
    Um = stack.enter_context(nc.sbuf_tensor(f"{tag}_Um", [128, 128], F32))
    identb = stack.enter_context(nc.sbuf_tensor(f"{tag}_identb", [128, 128], BF16))
    ones1 = stack.enter_context(nc.sbuf_tensor(f"{tag}_ones1", [1, 128], BF16))
    onescol = stack.enter_context(nc.sbuf_tensor(f"{tag}_onescol", [128, 1], BF16))
    grow = stack.enter_context(nc.sbuf_tensor(f"{tag}_grow", [1, DH], BF16))
    lngr = stack.enter_context(nc.sbuf_tensor(f"{tag}_lngr", [1, D], BF16))
    lnbr = stack.enter_context(nc.sbuf_tensor(f"{tag}_lnbr", [1, D], BF16))
    grms_b = stack.enter_context(nc.sbuf_tensor(f"{tag}_grmsb", [128, DH], F32))
    lng_b = stack.enter_context(nc.sbuf_tensor(f"{tag}_lngb", [128, D], F32))
    lnb_b = stack.enter_context(nc.sbuf_tensor(f"{tag}_lnbb", [128, D], F32))

    p.dma("sync", Lm[:], dr["Lmask"], f"{tag}_cl")
    p.dma("sync", Um[:], dr["Umask"], f"{tag}_cl")
    p.dma("sync", identb[:], dr["identb"], f"{tag}_cl")
    p.dma("sync", ones1[:], dr["ones1"], f"{tag}_cl")
    p.dma("sync", onescol[:], dr["onescol"], f"{tag}_cl")
    p.dma("gpsimd", grow[:], dr["g_rms_row"], f"{tag}_clg")
    p.dma("gpsimd", lngr[:], dr["lng_row"], f"{tag}_clg")
    p.dma("gpsimd", lnbr[:], dr["lnb_row"], f"{tag}_clg")

    def bc_tile(dst, src_row, width):
        for lo in range(0, width, 512):
            w = min(512, width - lo)
            p.op("tensor", lambda e, lo=lo, w=w, src_row=src_row: e.matmul(
                ps_b[:, 0:w], ones1[:], src_row[:, lo:lo + w], start=True, stop=True), f"{tag}_bc")
            p.wait("vector", f"{tag}_bc", p.cnt[f"{tag}_bc"])
            p.op("vector", lambda e, dst=dst, lo=lo, w=w: e.tensor_copy(
                dst[:, lo:lo + w], ps_b[:, 0:w]), f"{tag}_bce")
            p.wait("tensor", f"{tag}_bce", p.cnt[f"{tag}_bce"])

    with contextlib.ExitStack() as ps0:
        c["xlnT"] = ps0.enter_context(nc.sbuf_tensor(f"{tag}_xlnT", [128, DC * T_FULL], BF16))
        wsb = ps0.enter_context(nc.sbuf_tensor(f"{tag}_wsb", [128, DC * D], BF16))
        wb4 = ps0.enter_context(nc.sbuf_tensor(f"{tag}_wb4", [128, DC * 4], BF16))
        cwc = ps0.enter_context(nc.sbuf_tensor(f"{tag}_cwc", [128, 3 * 32], F32))
        urow = ps0.enter_context(nc.sbuf_tensor(f"{tag}_urow", [128, T_FULL + 4], F32))
        srow = ps0.enter_context(nc.sbuf_tensor(f"{tag}_srow", [128, 2 * T_FULL], F32))
        nrow = ps0.enter_context(nc.sbuf_tensor(f"{tag}_nrow", [1, T_FULL], F32))
        nrowb = ps0.enter_context(nc.sbuf_tensor(f"{tag}_nrowb", [1, T_FULL], BF16))
        bcr = ps0.enter_context(nc.sbuf_tensor(f"{tag}_bcr", [128, 512], F32))
        obf = ps0.enter_context(nc.sbuf_tensor(f"{tag}_obf", [128, T_FULL], BF16))
        ps_p = ps0.enter_context(nc.psum_tensor(f"{tag}_psP", [128, 2 * 512], F32))
        ps_n = ps0.enter_context(nc.psum_tensor(f"{tag}_psN", [128, 512], F32))
        ps_b = ps0.enter_context(nc.psum_tensor(f"{tag}_psB", [128, 512], F32))
        emit_outer_ln(p, c, dr["x_in"], dr["scr"], "ln")
        for i, nm in enumerate(("cq_cols", "ck_cols", "cv_cols")):
            p.dma("sync", cwc[:, i * 32:(i + 1) * 32], dr[nm], f"{tag}_cl")
        p.wait("tensor", f"{tag}_clg")
        p.wait("tensor", f"{tag}_cl")
        p.wait("tensor", "ln_tp")
        bc_tile(grms_b, grow, DH)
        bc_tile(lng_b, lngr, D)
        bc_tile(lnb_b, lnbr, D)
        p.dma("gpsimd", wb4[:].rearrange("p (ch n) -> p ch n", n=4),
              dr["Wb"].rearrange("(ch p) n -> p ch n", p=128), f"{tag}_wbl")
        p.wait("tensor", f"{tag}_wbl")
        for t in range(NT):
            if t >= 2:
                p.wait("tensor", f"{tag}_bev", t - 1)
            for ch in range(DC):
                p.op("tensor", lambda e, ch=ch, t=t: e.matmul(
                    ps_b[:, (t % 2) * 4:(t % 2) * 4 + 4],
                    xlnT_tile(c, ch, t), wb4[:, ch * 4:(ch + 1) * 4],
                    start=(ch == 0), stop=(ch == DC - 1)),
                    f"{tag}_bmm" if ch == DC - 1 else None)
            p.wait("scalar", f"{tag}_bmm", p.cnt[f"{tag}_bmm"])
            p.op("scalar", lambda e, t=t: e.activation(
                beta[:, t * 4:(t + 1) * 4], ps_b[:, (t % 2) * 4:(t % 2) * 4 + 4],
                AF.Sigmoid), f"{tag}_bev")
        p.wait("vector", f"{tag}_bev")
        p.op("vector", lambda e: e.tensor_scalar(nbeta[:], beta[:], -1.0, None, ALU.mult), f"{tag}_nbe")

        for ti, (wnm, dnm, cwi, need_norm) in enumerate(
                (("Wq", "Qfd", 0, True), ("Wk", "Kfd", 1, True), ("Wv", "Vfd", 2, False))):
            if ti >= 1:
                p.wait("gpsimd", f"{tag}_pmm", 64 * ti)
            p.dma("gpsimd", wsb[:].rearrange("p (ch n) -> p ch n", n=D),
                  dr[wnm].rearrange("(ch p) n -> p ch n", p=128), f"{tag}_wl{ti}")
            p.wait("tensor", f"{tag}_wl{ti}")
            for h in range(H):
                for j in range(2):
                    oc = 2 * h + j
                    ur = urow[:]
                    sr = srow[:, j * T_FULL:(j + 1) * T_FULL]
                    p.op("vector", lambda e, ur=ur: e.memset(ur[:, 0:4], 0.0), f"{tag}_pz")
                    for tg in range(8):
                        pp = ps_p[:, (tg % 2) * 512:(tg % 2) * 512 + 512]
                        if p.cnt.get(f"{tag}_pmm", 0) >= 2:
                            p.wait("tensor", f"{tag}_pev", p.cnt[f"{tag}_pmm"] - 1)
                        for ch in range(DC):
                            p.op("tensor", lambda e, ch=ch, oc=oc, tg=tg, pp=pp: e.matmul(
                                pp, wsb[:, ch * D + oc * 128:ch * D + (oc + 1) * 128],
                                xlnT_chunk(c, ch)[:, tg * 512:(tg + 1) * 512],
                                start=(ch == 0), stop=(ch == DC - 1)),
                                f"{tag}_pmm" if ch == DC - 1 else None)
                        p.wait("vector", f"{tag}_pmm", p.cnt[f"{tag}_pmm"])
                        p.op("vector", lambda e, ur=ur, tg=tg, pp=pp: e.tensor_copy(
                            ur[:, 4 + tg * 512: 4 + (tg + 1) * 512], pp), f"{tag}_pev")
                    if p.cnt.get(f"{tag}_sq", 0) > 0:
                        p.wait("vector", f"{tag}_sq", p.cnt[f"{tag}_sq"])
                    p.wait("vector", f"{tag}_pev", p.cnt[f"{tag}_pev"])
                    p.op("vector", lambda e, ur=ur, sr=sr, cwi=cwi, oc=oc: e.tensor_scalar(
                        sr, ur[:, 1:1 + T_FULL], cwc[:, cwi * 32 + oc * 4:cwi * 32 + oc * 4 + 1],
                        None, ALU.mult), f"{tag}_cv")
                    for i in (1, 2, 3):
                        p.wait("vector", f"{tag}_cv", p.cnt[f"{tag}_cv"])
                        p.op("vector", lambda e, ur=ur, sr=sr, cwi=cwi, oc=oc, i=i: e.scalar_tensor_tensor(
                            sr, ur[:, 1 + i:1 + i + T_FULL],
                            cwc[:, cwi * 32 + oc * 4 + i:cwi * 32 + oc * 4 + i + 1],
                            sr, ALU.mult, ALU.add), f"{tag}_cv")
                    p.wait("scalar", f"{tag}_cv", p.cnt[f"{tag}_cv"])
                    p.op("scalar", lambda e, sr=sr: e.activation(sr, sr, AF.Silu), f"{tag}_sl")
                if need_norm:
                    for j in range(2):
                        sr = srow[:, j * T_FULL:(j + 1) * T_FULL]
                        p.wait("scalar", f"{tag}_sl", p.cnt[f"{tag}_sl"])
                        if p.cnt.get(f"{tag}_nmm", 0) > 0:
                            p.wait("scalar", f"{tag}_nmm", p.cnt[f"{tag}_nmm"])
                        if p.cnt.get(f"{tag}_fst", 0) > 0:
                            p.wait("scalar", f"{tag}_fst", p.cnt[f"{tag}_fst"])
                        p.op("scalar", lambda e, sr=sr: e.activation(obf[:], sr, AF.Square), f"{tag}_sq")
                        p.wait("tensor", f"{tag}_sq", p.cnt[f"{tag}_sq"])
                        for tg in range(8):
                            if p.cnt.get(f"{tag}_nev", 0) > 0:
                                p.wait("tensor", f"{tag}_nev", p.cnt[f"{tag}_nev"])
                            p.op("tensor", lambda e, tg=tg: e.matmul(
                                ps_n[0:1, 0:512], onescol[:],
                                obf[:, tg * 512:(tg + 1) * 512],
                                start=True, stop=True), f"{tag}_nmm")
                            p.wait("vector", f"{tag}_nmm", p.cnt[f"{tag}_nmm"])
                            if j == 0:
                                p.op("vector", lambda e, tg=tg: e.tensor_copy(
                                    nrow[:, tg * 512:(tg + 1) * 512], ps_n[0:1, 0:512]), f"{tag}_nev")
                            else:
                                p.op("vector", lambda e, tg=tg: e.tensor_tensor(
                                    nrow[:, tg * 512:(tg + 1) * 512], ps_n[0:1, 0:512],
                                    nrow[:, tg * 512:(tg + 1) * 512], ALU.add), f"{tag}_nev")
                    p.wait("vector", f"{tag}_nev", p.cnt[f"{tag}_nev"])
                    p.op("vector", lambda e: e.tensor_scalar(nrow[:], nrow[:], 1e-6, None, ALU.add), f"{tag}_nr")
                    p.wait("scalar", f"{tag}_nr", p.cnt[f"{tag}_nr"])
                    p.op("scalar", lambda e: e.activation(nrow[:], nrow[:], AF.Ln), f"{tag}_nr")
                    p.wait("scalar", f"{tag}_nr", p.cnt[f"{tag}_nr"])
                    p.op("scalar", lambda e: e.activation(nrow[:], nrow[:], AF.Exp, scale=-0.5), f"{tag}_nr")
                    p.wait("scalar", f"{tag}_nr", p.cnt[f"{tag}_nr"])
                    p.op("scalar", lambda e: e.copy(nrowb[:], nrow[:]), f"{tag}_nr")
                    p.wait("tensor", f"{tag}_nr", p.cnt[f"{tag}_nr"])
                    p.wait("vector", f"{tag}_nr", p.cnt[f"{tag}_nr"])
                for j in range(2):
                    oc = 2 * h + j
                    sr = srow[:, j * T_FULL:(j + 1) * T_FULL]
                    if p.cnt.get(f"{tag}_fst", 0) > 0:
                        p.wait("vector", f"{tag}_fst", p.cnt[f"{tag}_fst"])
                    if need_norm:
                        p.wait("vector", f"{tag}_nr", p.cnt[f"{tag}_nr"])
                        for tg in range(8):
                            if p.cnt.get(f"{tag}_nbe", 0) > 0:
                                p.wait("tensor", f"{tag}_nbe", p.cnt[f"{tag}_nbe"])
                            p.op("tensor", lambda e, tg=tg: e.matmul(
                                ps_n[:, 0:512], ones1[:], nrowb[:, tg * 512:(tg + 1) * 512],
                                start=True, stop=True), f"{tag}_nbc")
                            p.wait("vector", f"{tag}_nbc", p.cnt[f"{tag}_nbc"])
                            p.op("vector", lambda e, tg=tg: e.tensor_copy(bcr[:], ps_n[:, 0:512]), f"{tag}_nbe")
                            p.wait("vector", f"{tag}_nbe", p.cnt[f"{tag}_nbe"])
                            p.op("vector", lambda e, sr=sr, tg=tg: e.tensor_tensor(
                                obf[:, tg * 512:(tg + 1) * 512], sr[:, tg * 512:(tg + 1) * 512],
                                bcr[:], ALU.mult), f"{tag}_ob")
                            p.wait("tensor", f"{tag}_ob", p.cnt[f"{tag}_ob"])
                    else:
                        p.wait("vector", f"{tag}_sl", p.cnt[f"{tag}_sl"])
                        p.op("vector", lambda e, sr=sr: e.tensor_copy(obf[:], sr), f"{tag}_ob")
                    p.wait("sync", f"{tag}_ob", p.cnt[f"{tag}_ob"])
                    p.dma("sync", dr[dnm].rearrange("(ch pp) t -> ch pp t", pp=128)[oc],
                          obf[:], f"{tag}_fst")
        for r in []:
            pass
        p.wait("sync", f"{tag}_fst")
        p.wait("vector", f"{tag}_nbe")
        p.op("vector", lambda e: e.engine_nop(), f"{tag}_Pdone")

    with contextlib.ExitStack() as ps1:
        Kt = ps1.enter_context(nc.sbuf_tensor(f"{tag}_Kt", [128, 2 * D], BF16))
        Qt = ps1.enter_context(nc.sbuf_tensor(f"{tag}_Qt", [128, 2 * D], BF16))
        Vt = ps1.enter_context(nc.sbuf_tensor(f"{tag}_Vt", [128, 2 * D], BF16))
        KT = ps1.enter_context(nc.sbuf_tensor(f"{tag}_KT", [128, 2 * D], BF16))
        QT = ps1.enter_context(nc.sbuf_tensor(f"{tag}_QT", [128, 2 * D], BF16))
        Ssb = ps1.enter_context(nc.sbuf_tensor(f"{tag}_Ssb", [128, H * 2 * DH], F32))
        Sbb = ps1.enter_context(nc.sbuf_tensor(f"{tag}_Sbb", [128, H * 2 * DH], BF16))
        Nsb = ps1.enter_context(nc.sbuf_tensor(f"{tag}_Nsb", [128, 128], BF16))
        Ntb = ps1.enter_context(nc.sbuf_tensor(f"{tag}_Ntb", [128, 128], BF16))
        Xb = ps1.enter_context(nc.sbuf_tensor(f"{tag}_Xb", [128, 128], BF16))
        Pb = ps1.enter_context(nc.sbuf_tensor(f"{tag}_Pb", [128, 128], BF16))
        Ptb = ps1.enter_context(nc.sbuf_tensor(f"{tag}_Ptb", [128, 128], BF16))
        Gb = ps1.enter_context(nc.sbuf_tensor(f"{tag}_Gb", [128, 128], BF16))
        Ab = ps1.enter_context(nc.sbuf_tensor(f"{tag}_Ab", [128, DH], BF16))
        Rb = ps1.enter_context(nc.sbuf_tensor(f"{tag}_Rb", [128, DH], BF16))
        tmp1 = ps1.enter_context(nc.sbuf_tensor(f"{tag}_tmp1", [128, DH], F32))
        rms1 = ps1.enter_context(nc.sbuf_tensor(f"{tag}_rms1", [128, 2], F32))
        Osb = ps1.enter_context(nc.sbuf_tensor(f"{tag}_Osb", [128, 2 * D], BF16))
        ps_kk = ps1.enter_context(nc.psum_tensor(f"{tag}_pkk", [128, 512], F32))
        ps_kq = ps1.enter_context(nc.psum_tensor(f"{tag}_pkq", [128, 512], F32))
        ps_db = ps1.enter_context(nc.psum_tensor(f"{tag}_pdb", [128, 512], F32))
        ps_tp = ps1.enter_context(nc.psum_tensor(f"{tag}_ptp", [128, 256], BF16))
        ps_R = ps1.enter_context(nc.psum_tensor(f"{tag}_pR", [128, 512], F32))
        ps_O = ps1.enter_context(nc.psum_tensor(f"{tag}_pO", [128, 512], F32))
        ps_S = ps1.enter_context(nc.psum_tensor(f"{tag}_pS", [128, 512], F32))
        for eng in ENGS:
            p.wait(eng, f"{tag}_Pdone")
        p.op("vector", lambda e: e.memset(Ssb[:], 0.0), f"{tag}_s0")
        p.op("vector", lambda e: e.memset(Sbb[:], 0.0), f"{tag}_s0")
        p.wait("tensor", f"{tag}_s0", p.cnt[f"{tag}_s0"])

        Qfv = dr["Qfd"].rearrange("(ch pp) t -> ch pp t", pp=128)
        Kfv = dr["Kfd"].rearrange("(ch pp) t -> ch pp t", pp=128)
        Vfv = dr["Vfd"].rearrange("(ch pp) t -> ch pp t", pp=128)

        def Sh(h, jj):
            return Ssb[:, (h * 2 + jj) * DH:(h * 2 + jj + 1) * DH]

        def Shb(h, jj):
            return Sbb[:, (h * 2 + jj) * DH:(h * 2 + jj + 1) * DH]

        for cc in range(NT):
            rb = cc % 2
            s_tl = f"{tag}_tl{rb}"
            kt = Kt[:, rb * D:(rb + 1) * D]
            qt = Qt[:, rb * D:(rb + 1) * D]
            vt = Vt[:, rb * D:(rb + 1) * D]
            kT = KT[:, rb * D:(rb + 1) * D]
            qT = QT[:, rb * D:(rb + 1) * D]
            if cc >= 2:
                p.wait("sync", f"{tag}_ch", 22 * H * (cc - 1))
                p.wait("gpsimd", f"{tag}_ch", 22 * H * (cc - 1))
                p.wait("vector", f"{tag}_ost{rb}", p.cnt[f"{tag}_ost{rb}"])
            p.dmat("sync", kt, dr["Kfd"][:, cc * 128:(cc + 1) * 128], s_tl)
            p.dmat("sync", qt, dr["Qfd"][:, cc * 128:(cc + 1) * 128], s_tl)
            p.dmat("sync", vt, dr["Vfd"][:, cc * 128:(cc + 1) * 128], s_tl)
            p.dma("gpsimd", kT.rearrange("p (ch t) -> p ch t", t=128),
                  Kfv[:, :, cc * 128:(cc + 1) * 128].rearrange("ch p t -> p ch t"), f"{tag}_tg{rb}")
            p.dma("gpsimd", qT.rearrange("p (ch t) -> p ch t", t=128),
                  Qfv[:, :, cc * 128:(cc + 1) * 128].rearrange("ch p t -> p ch t"), f"{tag}_tg{rb}")
            p.wait("tensor", s_tl, p.cnt[s_tl])
            p.wait("tensor", f"{tag}_tg{rb}", p.cnt[f"{tag}_tg{rb}"])
            for h in range(H):
                hi = cc * H + h
                if hi >= 1:
                    p.wait("tensor", f"{tag}_ch", 22 * hi)
                bcol = beta[:, cc * 4 + h:cc * 4 + h + 1]
                nbcol = nbeta[:, cc * 4 + h:cc * 4 + h + 1]
                for jj in range(2):
                    p.op("tensor", lambda e, h=h, jj=jj, kT=kT: e.matmul(
                        ps_kk[:, 0:128], kT[:, (2 * h + jj) * 128:(2 * h + jj + 1) * 128],
                        kT[:, (2 * h + jj) * 128:(2 * h + jj + 1) * 128],
                        start=(jj == 0), stop=(jj == 1)), None)
                for jj in range(2):
                    p.op("tensor", lambda e, h=h, jj=jj, kT=kT, qT=qT: e.matmul(
                        ps_kq[:, 0:128], kT[:, (2 * h + jj) * 128:(2 * h + jj + 1) * 128],
                        qT[:, (2 * h + jj) * 128:(2 * h + jj + 1) * 128],
                        start=(jj == 0), stop=(jj == 1)), f"{tag}_mm1" if jj == 1 else None)
                p.wait("vector", f"{tag}_mm1", p.cnt[f"{tag}_mm1"])
                p.op("vector", lambda e, nbcol=nbcol: e.tensor_scalar(
                    tmp1[:, 0:128], ps_kk[:, 0:128], nbcol, None, ALU.mult), f"{tag}_nb1")
                p.wait("vector", f"{tag}_nb1", p.cnt[f"{tag}_nb1"])
                p.op("vector", lambda e: e.tensor_tensor(Nsb[:], tmp1[:, 0:128], Lm[:], ALU.mult), f"{tag}_ch")
                p.op("vector", lambda e: e.tensor_tensor(Gb[:], ps_kq[:, 0:128], Um[:], ALU.mult), f"{tag}_ch")
                p.wait("tensor", f"{tag}_ch", p.cnt[f"{tag}_ch"])
                p.op("tensor", lambda e: e.transpose(ps_tp[:, 0:128], Nsb[:], identb[:]), f"{tag}_mm2")
                p.wait("vector", f"{tag}_mm2", p.cnt[f"{tag}_mm2"])
                p.op("vector", lambda e: e.tensor_copy(Ntb[:], ps_tp[:, 0:128]), f"{tag}_nb1")
                p.wait("vector", f"{tag}_nb1", p.cnt[f"{tag}_nb1"])
                p.op("vector", lambda e: e.tensor_tensor(Xb[:], identb[:], Ntb[:], ALU.add), None)
                p.op("vector", lambda e: e.tensor_copy(Pb[:], Nsb[:]), None)
                p.op("vector", lambda e: e.tensor_copy(Ptb[:], Ntb[:]), f"{tag}_ch")
                p.wait("tensor", f"{tag}_ch", p.cnt[f"{tag}_ch"])
                for lev in range(6):
                    p.op("tensor", lambda e: e.matmul(ps_db[:, 0:128], Ptb[:], Pb[:], start=True, stop=True), None)
                    p.op("tensor", lambda e: e.matmul(ps_db[:, 128:256], Pb[:], Ptb[:], start=True, stop=True), f"{tag}_mm3")
                    p.wait("vector", f"{tag}_mm3", p.cnt[f"{tag}_mm3"])
                    p.op("vector", lambda e: e.tensor_copy(Pb[:], ps_db[:, 0:128]), None)
                    p.op("vector", lambda e: e.tensor_copy(Ptb[:], ps_db[:, 128:256]), f"{tag}_ch", n=1)
                    p.wait("tensor", f"{tag}_ch", p.cnt[f"{tag}_ch"])
                    p.op("tensor", lambda e: e.matmul(ps_db[:, 256:384], Pb[:], Xb[:], start=True, stop=True), f"{tag}_mm3")
                    p.wait("vector", f"{tag}_mm3", p.cnt[f"{tag}_mm3"])
                    p.op("vector", lambda e: e.tensor_copy(tmp1[:, 0:128], ps_db[:, 256:384]), f"{tag}_xcp")
                    p.wait("vector", f"{tag}_xcp", p.cnt[f"{tag}_xcp"])
                    p.op("vector", lambda e: e.tensor_tensor(Xb[:], tmp1[:, 0:128], Xb[:], ALU.add), f"{tag}_ch")
                    p.wait("tensor", f"{tag}_ch", p.cnt[f"{tag}_ch"])
                for jj in range(2):
                    p.op("tensor", lambda e, h=h, jj=jj, kT=kT: e.matmul(
                        ps_R[:, 0:256], kT[:, (2 * h + jj) * 128:(2 * h + jj + 1) * 128],
                        Shb(h, jj), start=(jj == 0), stop=(jj == 1)), f"{tag}_mm4" if jj == 1 else None)
                p.wait("vector", f"{tag}_mm4", p.cnt[f"{tag}_mm4"])
                p.op("vector", lambda e, h=h, vt=vt: e.tensor_tensor(
                    tmp1[:], vt[:, h * DH:(h + 1) * DH], ps_R[:, 0:256], ALU.subtract), f"{tag}_rr")
                p.wait("vector", f"{tag}_rr", p.cnt[f"{tag}_rr"])
                p.op("vector", lambda e, bcol=bcol: e.tensor_scalar(
                    Rb[:], tmp1[:], bcol, None, ALU.mult), f"{tag}_ch")
                p.wait("tensor", f"{tag}_ch", p.cnt[f"{tag}_ch"])
                p.op("tensor", lambda e: e.matmul(ps_R[:, 256:512], Xb[:], Rb[:], start=True, stop=True), f"{tag}_mm5")
                p.wait("vector", f"{tag}_mm5", p.cnt[f"{tag}_mm5"])
                p.op("vector", lambda e: e.tensor_copy(Ab[:], ps_R[:, 256:512]), f"{tag}_ch")
                p.wait("tensor", f"{tag}_ch", p.cnt[f"{tag}_ch"])
                for jj in range(2):
                    p.op("tensor", lambda e, h=h, jj=jj, qT=qT: e.matmul(
                        ps_O[:, 0:256], qT[:, (2 * h + jj) * 128:(2 * h + jj + 1) * 128],
                        Shb(h, jj), start=(jj == 0), stop=False), None)
                p.op("tensor", lambda e: e.matmul(
                    ps_O[:, 0:256], Gb[:], Ab[:], start=False, stop=True), f"{tag}_mm6")
                for jj in range(2):
                    p.op("tensor", lambda e, h=h, jj=jj, kt=kt: e.matmul(
                        ps_S[:, jj * 256:(jj + 1) * 256], kt[:, h * DH + jj * 128: h * DH + (jj + 1) * 128],
                        Ab[:], start=True, stop=True), f"{tag}_mm7" if jj == 1 else None)
                p.wait("vector", f"{tag}_mm7", p.cnt[f"{tag}_mm7"])
                for jj in range(2):
                    p.op("vector", lambda e, h=h, jj=jj: e.tensor_tensor(
                        Sh(h, jj), ps_S[:, jj * 256:(jj + 1) * 256], Sh(h, jj), ALU.add), f"{tag}_ch")
                p.wait("vector", f"{tag}_ch", p.cnt[f"{tag}_ch"])
                for jj in range(2):
                    p.op("vector", lambda e, h=h, jj=jj: e.tensor_copy(Shb(h, jj), Sh(h, jj)), f"{tag}_ch")
                p.wait("scalar", f"{tag}_mm6", p.cnt[f"{tag}_mm6"])
                p.op("scalar", lambda e: e.activation(
                    tmp1[:], ps_O[:, 0:256], AF.Square, accum_out=rms1[:, 0:1]), f"{tag}_rm")
                p.wait("vector", f"{tag}_rm", p.cnt[f"{tag}_rm"])
                p.op("vector", lambda e: e.tensor_scalar(
                    rms1[:, 1:2], rms1[:, 0:1], 1.0 / DH, 1e-5, ALU.mult, ALU.add), f"{tag}_rm")
                p.wait("scalar", f"{tag}_rm", p.cnt[f"{tag}_rm"])
                p.op("scalar", lambda e: e.activation(rms1[:, 1:2], rms1[:, 1:2], AF.Ln), f"{tag}_rm")
                p.wait("scalar", f"{tag}_rm", p.cnt[f"{tag}_rm"])
                p.op("scalar", lambda e: e.activation(rms1[:, 1:2], rms1[:, 1:2], AF.Exp, scale=-0.5), f"{tag}_rm")
                p.wait("vector", f"{tag}_rm", p.cnt[f"{tag}_rm"])
                p.op("vector", lambda e: e.tensor_scalar(
                    tmp1[:], ps_O[:, 0:256], rms1[:, 1:2], None, ALU.mult), f"{tag}_on")
                p.wait("vector", f"{tag}_on", p.cnt[f"{tag}_on"])
                p.op("vector", lambda e, h=h, rb=rb: e.tensor_tensor(
                    Osb[:, rb * D + h * DH: rb * D + (h + 1) * DH], tmp1[:], grms_b[:], ALU.mult),
                    f"{tag}_ch")
            p.wait("sync", f"{tag}_ch")
            if cc >= 2:
                p.wait("sync", f"{tag}_ost{rb}", p.cnt[f"{tag}_ost{rb}"])
            p.dma("sync", dr["Od"].rearrange("(t pp) d -> t pp d", pp=128)[cc],
                  Osb[:, rb * D:(rb + 1) * D], f"{tag}_ost{rb}")

    with contextlib.ExitStack() as ps2:
        projw = ps2.enter_context(nc.sbuf_tensor(f"{tag}_projw", [128, DC * D], BF16))
        OTsb = ps2.enter_context(nc.sbuf_tensor(f"{tag}_OTsb", [128, 2 * DC * 512], BF16))
        xlf = ps2.enter_context(nc.sbuf_tensor(f"{tag}_xlf", [128, 2 * D], F32))
        yv = ps2.enter_context(nc.sbuf_tensor(f"{tag}_yv", [128, 2 * D], F32))
        bny = ps2.enter_context(nc.sbuf_tensor(f"{tag}_bny", [128, 16], F32))
        ps_pr = ps2.enter_context(nc.psum_tensor(f"{tag}_psPr", [128, 1024], F32))
        for r in range(2):
            p.wait("gpsimd", f"{tag}_ost{r}")
            p.wait("sync", f"{tag}_ost{r}")
        p.dma("gpsimd", projw[:].rearrange("p (ch n) -> p ch n", n=D),
              dr["Wo"].rearrange("(ch p) n -> p ch n", p=128), f"{tag}_pw")
        p.wait("tensor", f"{tag}_pw")
        for g in range(8):
            rb = g % 2
            if g >= 2:
                p.wait("sync", f"{tag}_prm", 4 * (g - 1))
            for ch in range(DC):
                p.dmat("sync", ot_view(OTsb, rb, ch),
                       dr["Od"][g * 512:(g + 1) * 512, ch * 128:(ch + 1) * 128], f"{tag}_ot{rb}")
            p.wait("tensor", f"{tag}_ot{rb}", p.cnt[f"{tag}_ot{rb}"])
            for s in range(4):
                gt = g * 4 + s
                rr = gt % 2
                if gt >= 1:
                    p.wait("tensor", f"{tag}_yev", 1 * gt)
                for half in range(2):
                    for ch in range(DC):
                        p.op("tensor", lambda e, s=s, half=half, ch=ch, rb=rb: e.matmul(
                            ps_pr[:, half * 512:(half + 1) * 512],
                            ot_view(OTsb, rb, ch)[:, s * 128:(s + 1) * 128],
                            projw[:, ch * D + half * 512: ch * D + (half + 1) * 512],
                            start=(ch == 0), stop=(ch == DC - 1)),
                            f"{tag}_prm" if (half == 1 and ch == DC - 1) else None)
                r4 = gt % 4
                xt = c["xrot"][:, r4 * D:(r4 + 1) * D]
                if gt < 4:
                    p.wait("sync", "ln_app", NT)
                else:
                    p.wait("sync", f"{tag}_xst2", p.cnt.get(f"{tag}_xst2", 0))
                    p.wait("sync", f"{tag}_yapp", gt - 3)
                vld = p.dma("sync", xt, dr["x_in"].rearrange("(t p) d -> t p d", p=128)[gt], f"{tag}_xl{r4}")
                p.wait("vector", f"{tag}_xl{r4}", vld)
                p.op("vector", lambda e, gt=gt, xt=xt, rr=rr: e.tensor_scalar(
                    xlf[:, rr * D:(rr + 1) * D], xt, ln_mean(c, gt), ln_rstd(c, gt),
                    ALU.subtract, ALU.mult), f"{tag}_yapp")
                p.wait("vector", f"{tag}_prm", p.cnt[f"{tag}_prm"])
                p.op("vector", lambda e, rr=rr: e.tensor_tensor(
                    yv[:, rr * D:(rr + 1) * D], ps_pr[:], xlf[:, rr * D:(rr + 1) * D], ALU.add), f"{tag}_yev")
                p.wait("vector", f"{tag}_yev", p.cnt[f"{tag}_yev"])
                p.op("vector", lambda e, rr=rr: e.bn_stats(bny[:, 0:6], yv[:, rr * D:rr * D + 512]))
                p.op("vector", lambda e, rr=rr: e.bn_stats(bny[:, 6:12], yv[:, rr * D + 512:rr * D + 1024]), f"{tag}_bn")
                p.wait("vector", f"{tag}_bn", p.cnt[f"{tag}_bn"])
                p.op("vector", lambda e: e.bn_aggr(bny[:, 12:14], bny[:, 0:12]), f"{tag}_bn")
                p.wait("vector", f"{tag}_bn", p.cnt[f"{tag}_bn"])
                p.op("vector", lambda e: e.tensor_scalar(bny[:, 13:14], bny[:, 13:14], 1e-5, None, ALU.add), f"{tag}_bn")
                p.wait("scalar", f"{tag}_bn", p.cnt[f"{tag}_bn"])
                p.op("scalar", lambda e: e.activation(bny[:, 13:14], bny[:, 13:14], AF.Ln), f"{tag}_rs")
                p.wait("scalar", f"{tag}_rs", p.cnt[f"{tag}_rs"])
                p.op("scalar", lambda e: e.activation(bny[:, 13:14], bny[:, 13:14], AF.Exp, scale=-0.5), f"{tag}_rs")
                p.wait("vector", f"{tag}_rs", p.cnt[f"{tag}_rs"])
                p.op("vector", lambda e, rr=rr: e.tensor_scalar(
                    yv[:, rr * D:(rr + 1) * D], yv[:, rr * D:(rr + 1) * D],
                    bny[:, 12:13], bny[:, 13:14], ALU.subtract, ALU.mult), f"{tag}_y2")
                p.wait("vector", f"{tag}_y2", p.cnt[f"{tag}_y2"])
                p.op("vector", lambda e, rr=rr: e.tensor_tensor(
                    yv[:, rr * D:(rr + 1) * D], yv[:, rr * D:(rr + 1) * D], lng_b[:], ALU.mult), f"{tag}_y3")
                p.wait("vector", f"{tag}_y3", p.cnt[f"{tag}_y3"])
                p.op("vector", lambda e, rr=rr: e.tensor_tensor(
                    yv[:, rr * D:(rr + 1) * D], yv[:, rr * D:(rr + 1) * D], lnb_b[:], ALU.add), f"{tag}_y4")
                p.wait("vector", f"{tag}_y4", p.cnt[f"{tag}_y4"])
                p.op("vector", lambda e, rr=rr, xt=xt: e.tensor_tensor(
                    xt, yv[:, rr * D:(rr + 1) * D], xt, ALU.add), f"{tag}_y5")
                p.wait("sync", f"{tag}_y5", p.cnt[f"{tag}_y5"])
                p.dma("sync", dr["x_out"].rearrange("(t p) d -> t p d", p=128)[gt], xt, f"{tag}_xst2")
        p.wait("sync", f"{tag}_xst2")


def l1b_consts():
    NEG = 0.0
    k = np.arange(128)
    Lm = (k[:, None] > k[None, :]).astype(np.float32)
    Um = (k[:, None] <= k[None, :]).astype(np.float32)
    return Lm, Um


def build_l1_attn_b():
    nc = bass.Bass("TRN2", target_bir_lowering=False)
    x_in = nc.dram_tensor("x_in", [T_FULL, D], F32, kind="ExternalInput").ap()
    Wq = nc.dram_tensor("Wq", [D, D], F32, kind="ExternalInput").ap()
    Wk = nc.dram_tensor("Wk", [D, D], F32, kind="ExternalInput").ap()
    Wv = nc.dram_tensor("Wv", [D, D], F32, kind="ExternalInput").ap()
    Wb = nc.dram_tensor("Wb", [D, 4], F32, kind="ExternalInput").ap()
    Wo = nc.dram_tensor("Wo", [D, D], F32, kind="ExternalInput").ap()
    cq = nc.dram_tensor("cq_cols", [128, 32], F32, kind="ExternalInput").ap()
    ck = nc.dram_tensor("ck_cols", [128, 32], F32, kind="ExternalInput").ap()
    cv = nc.dram_tensor("cv_cols", [128, 32], F32, kind="ExternalInput").ap()
    g_rms_row = nc.dram_tensor("g_rms_row", [1, 256], F32, kind="ExternalInput").ap()
    lng_row = nc.dram_tensor("lng_row", [1, D], F32, kind="ExternalInput").ap()
    lnb_row = nc.dram_tensor("lnb_row", [1, D], F32, kind="ExternalInput").ap()
    ones1 = nc.dram_tensor("ones1", [1, 128], BF16, kind="ExternalInput").ap()
    onescol = nc.dram_tensor("onescol", [128, 1], BF16, kind="ExternalInput").ap()
    identb = nc.dram_tensor("identb", [128, 128], BF16, kind="ExternalInput").ap()
    Lmask = nc.dram_tensor("Lmask", [128, 128], F32, kind="ExternalInput").ap()
    Umask = nc.dram_tensor("Umask", [128, 128], F32, kind="ExternalInput").ap()
    x_out = nc.dram_tensor("x_out", [T_FULL, D], F32, kind="ExternalOutput").ap()
    scr = nc.dram_tensor("scr", [T_FULL, D], BF16, kind="Internal").ap()
    Qfd = nc.dram_tensor("Qfd", [D, T_FULL], BF16, kind="Internal").ap()
    Kfd = nc.dram_tensor("Kfd", [D, T_FULL], BF16, kind="Internal").ap()
    Vfd = nc.dram_tensor("Vfd", [D, T_FULL], BF16, kind="Internal").ap()
    Od = nc.dram_tensor("Od", [T_FULL, D], BF16, kind="Internal").ap()
    with contextlib.ExitStack() as stack:
        p = Prog(nc, stack)
        c = alloc_ln(nc, stack, with_xlnT=False)
        dr = dict(Wq=Wq, Wk=Wk, Wv=Wv, Wb=Wb, Wo=Wo, cq_cols=cq, ck_cols=ck,
                  cv_cols=cv, g_rms_row=g_rms_row, lng_row=lng_row,
                  lnb_row=lnb_row, ones1=ones1, onescol=onescol, identb=identb,
                  Lmask=Lmask, Umask=Umask, x_in=x_in, x_out=x_out, scr=scr,
                  Qfd=Qfd, Kfd=Kfd, Vfd=Vfd, Od=Od)
        emit_l1_attn_b(p, nc, c, stack, dr)
        p.check_deadlock()
        p.realize()
    return nc



A0_GROUPS = (("mx", "mxb", "mx2", "zz", "zm", "zr", "zb", "zc"),
             ("cp", "sl", "rmm", "pf", "qmm", "ri"),
             ("cmm", "scv", "bmm", "bev"),
             ("iw", "wrb", "wt"),
             ("hmm", "smm", "sev"),
             ("mb", "mbe"),
             ("mbn", "mrs", "map"),
             ("n1s0", "n2s0"),
             ("n1t", "n2t"),
             ("bcm", "srw", "sbm", "sbe", "ym", "ym2"),
             ("gmm", "gpr", "gt"),
             ("w1l0", "w2l0"),
             ("w1l1", "w2l1"),
             ("gb", "gb2", "gb3", "bmm2"),
             ("fw", "fw2"))


def _a0_alias(p, tag):
    for grp in A0_GROUPS:
        for nm in grp[1:]:
            p.alias[f"{tag}_{nm}"] = f"{tag}_{grp[0]}"


def emit_l0_attn_a_p1(p, nc, stack, dr, tag="a0"):
    _a0_alias(p, tag)
    K2 = 2048
    CT = 16

    onesf = stack.enter_context(nc.sbuf_tensor(f"{tag}_onesf", [1, 128], F32))
    onescf = stack.enter_context(nc.sbuf_tensor(f"{tag}_onescf", [128, 1], F32))
    one1f = stack.enter_context(nc.sbuf_tensor(f"{tag}_one1f", [1, 1], F32))
    identf = stack.enter_context(nc.sbuf_tensor(f"{tag}_identf", [128, 128], F32))
    cstl = stack.enter_context(nc.sbuf_tensor(f"{tag}_cstl", [128, 128], F32))
    scol = stack.enter_context(nc.sbuf_tensor(f"{tag}_scol", [128, NT], F32))
    rank = stack.enter_context(nc.sbuf_tensor(f"{tag}_rank", [128, NT], F32))
    sel = stack.enter_context(nc.sbuf_tensor(f"{tag}_sel", [128, NT], F32))
    ridx = stack.enter_context(nc.sbuf_tensor(f"{tag}_ridx", [128, NT], F32))
    iwork = stack.enter_context(nc.sbuf_tensor(f"{tag}_iwork", [128, 4 * NT], I32))
    ework = stack.enter_context(nc.sbuf_tensor(f"{tag}_ework", [128, 2 * NT], F32))
    prow = stack.enter_context(nc.sbuf_tensor(f"{tag}_prow", [1, 2 * NT], F32))
    srow = stack.enter_context(nc.sbuf_tensor(f"{tag}_srow", [1, T_FULL], F32))
    zsc = stack.enter_context(nc.sbuf_tensor(f"{tag}_zsc", [128, 8], F32))
    wtile = stack.enter_context(nc.sbuf_tensor(f"{tag}_wtile", [128, CT], F32))

    p.dma("sync", onesf[:], dr["onesf"], f"{tag}_cl")
    p.dma("sync", onescf[:], dr["onescf"], f"{tag}_cl")
    p.dma("sync", one1f[:], dr["one1f"], f"{tag}_cl")
    p.dma("sync", identf[:], dr["identf"], f"{tag}_cl")
    p.dma("sync", cstl[:], dr["cstl"], f"{tag}_cl")

    with contextlib.ExitStack() as ps0:
        c = alloc_ln(nc, ps0, with_xlnT=False)
        c["lrotf"] = ps0.enter_context(nc.sbuf_tensor(f"{tag}_lrotf", [128, 2 * D], F32))
        w2f = ps0.enter_context(nc.sbuf_tensor(f"{tag}_w2f", [128, 2], F32))
        b2s = ps0.enter_context(nc.sbuf_tensor(f"{tag}_b2s", [1, 1], F32))
        ps_t = ps0.enter_context(nc.psum_tensor(f"{tag}_psT", [128, 1024], F32))
        ps_h = ps0.enter_context(nc.psum_tensor(f"{tag}_psH", [128, 2 * 512], F32))
        ps_r = ps0.enter_context(nc.psum_tensor(f"{tag}_psR", [128, 512], F32))
        ps_s = ps0.enter_context(nc.psum_tensor(f"{tag}_psS", [128, 512], F32))
        ps05 = contextlib.ExitStack()
        hrT = ps05.enter_context(nc.sbuf_tensor(f"{tag}_hrT", [128, 2 * T_FULL], F32))
        xfr = ps05.enter_context(nc.sbuf_tensor(f"{tag}_xfr", [128, 2 * D], F32))
        xtf = ps05.enter_context(nc.sbuf_tensor(f"{tag}_xtf", [128, 2 * 4096], F32))
        w1f = ps05.enter_context(nc.sbuf_tensor(f"{tag}_w1f", [128, DC * 256], F32))
        b1c = ps05.enter_context(nc.sbuf_tensor(f"{tag}_b1c", [128, 2], F32))
        emit_outer_ln(p, c, dr["x_in"], dr["scr"], "ln", transpose=False,
                      scrF=dr["scrF"])
        p.dma("gpsimd", w1f[:].rearrange("p (ch n) -> p ch n", n=256),
              dr["modr_w1"].rearrange("(ch p) n -> p ch n", p=128), f"{tag}_wl")
        p.dma("sync", w2f[:], dr["modr_w2_cols"], f"{tag}_cl")
        p.dma("sync", b1c[:], dr["modr_b1_cols"], f"{tag}_cl")
        p.dma("sync", b2s[:], dr["modr_b2"], f"{tag}_cl")
        p.wait("tensor", f"{tag}_cl")
        p.wait("tensor", f"{tag}_wl")
        xfsrc = dr["scrF"].rearrange("(t p) d -> t p d", p=128)
        for tg in range(8):
            rb = tg % 2
            xv = xtf[:, rb * 4096:(rb + 1) * 4096]
            for tt in range(4):
                t = tg * 4 + tt
                rr = t % 2
                xf = xfr[:, rr * D:(rr + 1) * D]
                s_xf = f"{tag}_xf{rr}"
                if t >= 2:
                    p.wait("sync", f"{tag}_tpe", 8 * (t - 1))
                vld = p.dma("sync", xf, xfsrc[t], s_xf)
                p.wait("tensor", s_xf, vld)
                if t >= 2:
                    p.wait("tensor", f"{tag}_tpe", 8 * (t - 1))
                for j in range(DC):
                    if p.cnt.get(f"{tag}_tpe", 0) >= 2:
                        p.wait("tensor", f"{tag}_tpe", p.cnt[f"{tag}_tpe"] - 1)
                    p.op("tensor", lambda e, xf=xf, j=j: e.transpose(
                        ps_t[:, (j % 2) * 512:(j % 2) * 512 + 128],
                        xf[:, j * 128:(j + 1) * 128], identf[:]),
                        f"{tag}_tpp")
                    p.wait("vector", f"{tag}_tpp", p.cnt[f"{tag}_tpp"])
                    p.op("vector", lambda e, xv=xv, j=j, tt=tt: e.tensor_copy(
                        xv.rearrange("p (j t) -> p j t", j=8)[:, j, tt * 128:(tt + 1) * 128],
                        ps_t[:, (j % 2) * 512:(j % 2) * 512 + 128]), f"{tag}_tpe")
            p.wait("tensor", f"{tag}_tpe", p.cnt[f"{tag}_tpe"])
            if tg >= 1:
                p.wait("tensor", f"{tag}_hev", 2 * tg)
            for m in range(2):
                for j in range(DC):
                    p.op("tensor", lambda e, m=m, j=j, xv=xv: e.matmul(
                        ps_h[:, m * 512:(m + 1) * 512],
                        w1f[:, j * 256 + m * 128: j * 256 + (m + 1) * 128],
                        xv.rearrange("p (j t) -> p j t", j=8)[:, j],
                        start=(j == 0), stop=(j == DC - 1)),
                        f"{tag}_hmm" if j == DC - 1 else None)
            p.wait("scalar", f"{tag}_hmm", p.cnt[f"{tag}_hmm"])
            for m in range(2):
                p.op("scalar", lambda e, m=m, tg=tg: e.activation(
                    hrT[:, m * T_FULL + tg * 512: m * T_FULL + (tg + 1) * 512],
                    ps_h[:, m * 512:(m + 1) * 512], AF.Relu, bias=b1c[:, m:m + 1]),
                    f"{tag}_hev")
        p.wait("tensor", f"{tag}_hev")
        for tg in range(8):
            for m in range(2):
                p.op("tensor", lambda e, m=m, tg=tg: e.matmul(
                    ps_s[0:1, 0:512], w2f[:, m:m + 1],
                    hrT[:, m * T_FULL + tg * 512: m * T_FULL + (tg + 1) * 512],
                    start=(m == 0), stop=(m == 1)), f"{tag}_smm" if m == 1 else None)
            p.wait("vector", f"{tag}_smm", p.cnt[f"{tag}_smm"])
            p.op("vector", lambda e, tg=tg: e.tensor_scalar(
                srow[:, tg * 512:(tg + 1) * 512], ps_s[0:1, 0:512],
                b2s[0:1, 0:1], None, ALU.add), f"{tag}_sev")
            p.wait("tensor", f"{tag}_sev", p.cnt[f"{tag}_sev"])
        for eng in ("vector", "tensor", "scalar", "sync", "gpsimd"):
            p.wait(eng, f"{tag}_sev")
        ps05.close()
        sbc = ps0.enter_context(nc.sbuf_tensor(f"{tag}_sbc", [128, T_FULL], F32))
        cmp = ps0.enter_context(nc.sbuf_tensor(f"{tag}_cmp", [128, T_FULL], F32))
        for t in range(NT):
            p.op("tensor", lambda e, t=t: e.matmul(
                ps_r[:, t:t + 1] if t < 512 else None,
                srow[0:1, t * 128:(t + 1) * 128], one1f[:],
                start=True, stop=True), f"{tag}_cmm" if t == NT - 1 else None)
        p.wait("vector", f"{tag}_cmm", p.cnt[f"{tag}_cmm"])
        p.op("vector", lambda e: e.tensor_copy(scol[:], ps_r[:, 0:NT]), f"{tag}_scv")
        p.wait("tensor", f"{tag}_scv", p.cnt[f"{tag}_scv"])
        for tg in range(8):
            p.op("tensor", lambda e, tg=tg: e.matmul(
                ps_s[:, 0:512], onesf[:], srow[0:1, tg * 512:(tg + 1) * 512],
                start=True, stop=True), f"{tag}_bmm")
            p.wait("vector", f"{tag}_bmm", p.cnt[f"{tag}_bmm"])
            p.op("vector", lambda e, tg=tg: e.tensor_copy(
                sbc[:, tg * 512:(tg + 1) * 512], ps_s[:, 0:512]), f"{tag}_bev")
            p.wait("tensor", f"{tag}_bev", p.cnt[f"{tag}_bev"])
        p.wait("vector", f"{tag}_bev")
        for ct in range(NT):
            p.op("vector", lambda e, ct=ct: e.tensor_scalar(
                cmp[:], sbc[:], scol[:, ct:ct + 1], None, ALU.is_gt), f"{tag}_cp")
            p.wait("vector", f"{tag}_cp", p.cnt[f"{tag}_cp"])
            p.op("vector", lambda e, ct=ct: e.tensor_reduce(
                rank[:, ct:ct + 1], cmp[:], AX.X, ALU.add), f"{tag}_cp")
            p.wait("vector", f"{tag}_cp", p.cnt[f"{tag}_cp"])
        p.op("vector", lambda e: e.tensor_scalar(sel[:], rank[:], float(K2), None, ALU.is_lt), f"{tag}_sl")
        p.wait("vector", f"{tag}_sl", p.cnt[f"{tag}_sl"])
        p.wait("tensor", f"{tag}_sl")
        p.op("tensor", lambda e: e.matmul(ps_r[:, 0:NT], cstl[:], sel[:], start=True, stop=True), f"{tag}_rmm")
        p.op("tensor", lambda e: e.matmul(ps_s[0:1, 0:NT], onescf[:], sel[:], start=True, stop=True), f"{tag}_rmm")
        p.wait("vector", f"{tag}_rmm", p.cnt[f"{tag}_rmm"])
        p.op("vector", lambda e: e.tensor_copy(prow[:, 0:NT], ps_s[0:1, 0:NT]), f"{tag}_pf")
        p.wait("vector", f"{tag}_pf", p.cnt[f"{tag}_pf"])
        p.op("vector", lambda e: e.memset(prow[:, NT:NT + 1], 0.0), f"{tag}_pf")
        p.op("vector", lambda e: e.tensor_copy(prow[:, NT + 1:2 * NT], prow[:, 0:NT - 1]), f"{tag}_pf")
        p.wait("vector", f"{tag}_pf", p.cnt[f"{tag}_pf"])
        for sh in (1, 2, 4, 8, 16):
            p.op("vector", lambda e, sh=sh: e.tensor_tensor(
                prow[:, NT + sh:2 * NT], prow[:, NT + sh:2 * NT],
                prow[:, NT:2 * NT - sh], ALU.add), f"{tag}_pf")
            p.wait("vector", f"{tag}_pf", p.cnt[f"{tag}_pf"])
        p.wait("tensor", f"{tag}_pf")
        p.op("tensor", lambda e: e.matmul(ps_s[:, 0:NT], onesf[:], prow[0:1, NT:2 * NT], start=True, stop=True), f"{tag}_qmm")
        p.wait("vector", f"{tag}_qmm", p.cnt[f"{tag}_qmm"])
        p.op("vector", lambda e: e.tensor_copy(ework[:, 0:NT], ps_s[:, 0:NT]), f"{tag}_ri")
        p.wait("vector", f"{tag}_ri", p.cnt[f"{tag}_ri"])
        p.op("vector", lambda e: e.tensor_tensor(ridx[:], ps_r[:, 0:NT], ework[:, 0:NT], ALU.add), f"{tag}_ri")
        p.wait("vector", f"{tag}_ri", p.cnt[f"{tag}_ri"])
        p.op("vector", lambda e: e.tensor_reduce(zsc[0:1, 0:1], srow[:], AX.X, ALU.max), f"{tag}_mx")
        p.wait("vector", f"{tag}_mx", p.cnt[f"{tag}_mx"])
        p.op("vector", lambda e: e.tensor_scalar(zsc[0:1, 1:2], zsc[0:1, 0:1], -1.0, None, ALU.mult), f"{tag}_mx")
        p.wait("tensor", f"{tag}_mx", p.cnt[f"{tag}_mx"])
        p.op("tensor", lambda e: e.matmul(ps_s[:, 0:1], onesf[:], zsc[0:1, 1:2], start=True, stop=True), f"{tag}_mxb")
        p.wait("vector", f"{tag}_mxb", p.cnt[f"{tag}_mxb"])
        p.op("vector", lambda e: e.tensor_copy(zsc[:, 2:3], ps_s[:, 0:1]), f"{tag}_mx2")
        p.wait("scalar", f"{tag}_mx2", p.cnt[f"{tag}_mx2"])
        p.op("scalar", lambda e: e.activation(ework[:, 0:NT], scol[:], AF.Exp, bias=zsc[:, 2:3]), f"{tag}_ex")
        p.wait("vector", f"{tag}_ex", p.cnt[f"{tag}_ex"])
        p.op("vector", lambda e: e.tensor_tensor(ework[:, NT:2 * NT], ework[:, 0:NT], sel[:], ALU.mult), f"{tag}_ex")
        p.wait("vector", f"{tag}_ex", p.cnt[f"{tag}_ex"])
        p.op("vector", lambda e: e.tensor_reduce(zsc[:, 3:4], ework[:, NT:2 * NT], AX.X, ALU.add), f"{tag}_zz")
        p.wait("tensor", f"{tag}_zz", p.cnt[f"{tag}_zz"])
        p.op("tensor", lambda e: e.matmul(ps_s[0:1, 1:2], zsc[:, 3:4], onescf[:], start=True, stop=True), f"{tag}_zm")
        p.wait("vector", f"{tag}_zm", p.cnt[f"{tag}_zm"])
        p.op("vector", lambda e: e.reciprocal(zsc[0:1, 4:5], ps_s[0:1, 1:2]), f"{tag}_zr")
        p.wait("tensor", f"{tag}_zr", p.cnt[f"{tag}_zr"])
        p.op("tensor", lambda e: e.matmul(ps_s[:, 2:3], onesf[:], zsc[0:1, 4:5], start=True, stop=True), f"{tag}_zb")
        p.wait("vector", f"{tag}_zb", p.cnt[f"{tag}_zb"])
        p.op("vector", lambda e: e.tensor_copy(zsc[:, 5:6], ps_s[:, 2:3]), f"{tag}_zc")
        p.wait("vector", f"{tag}_zc", p.cnt[f"{tag}_zc"])
        p.op("vector", lambda e: e.tensor_copy(iwork[:, 0:NT], rank[:]), f"{tag}_iw")
        p.op("vector", lambda e: e.tensor_copy(iwork[:, NT:2 * NT], ridx[:]), f"{tag}_iw")
        p.wait("vector", f"{tag}_iw", p.cnt[f"{tag}_iw"])
        for base in (0, NT):
            p.op("vector", lambda e, base=base: e.tensor_scalar(
                iwork[:, 2 * NT:3 * NT], iwork[:, base:base + NT], 127, None, ALU.bitwise_and), f"{tag}_iw")
            p.op("vector", lambda e, base=base: e.tensor_scalar(
                iwork[:, 3 * NT:4 * NT], iwork[:, base:base + NT], 7, None, ALU.arith_shift_right), f"{tag}_iw")
            p.wait("vector", f"{tag}_iw", p.cnt[f"{tag}_iw"])
            p.op("vector", lambda e: e.tensor_scalar(
                iwork[:, 2 * NT:3 * NT], iwork[:, 2 * NT:3 * NT], 4, None, ALU.arith_shift_left), f"{tag}_iw")
            p.wait("vector", f"{tag}_iw", p.cnt[f"{tag}_iw"])
            p.op("vector", lambda e, base=base: e.tensor_tensor(
                iwork[:, base:base + NT], iwork[:, 2 * NT:3 * NT], iwork[:, 3 * NT:4 * NT], ALU.add), f"{tag}_iw")
            p.wait("vector", f"{tag}_iw", p.cnt[f"{tag}_iw"])
        p.op("vector", lambda e: e.tensor_scalar(ework[:, 0:NT], sel[:], -65536.0, 65536.0, ALU.mult, ALU.add), f"{tag}_iw")
        p.wait("vector", f"{tag}_iw", p.cnt[f"{tag}_iw"])
        p.op("vector", lambda e: e.tensor_copy(iwork[:, 2 * NT:3 * NT], ework[:, 0:NT]), f"{tag}_iw")
        p.wait("vector", f"{tag}_iw", p.cnt[f"{tag}_iw"])
        p.op("vector", lambda e: e.tensor_tensor(iwork[:, NT:2 * NT], iwork[:, NT:2 * NT], iwork[:, 2 * NT:3 * NT], ALU.add), f"{tag}_iw")
        p.wait("gpsimd", f"{tag}_iw")
        p.wait("gpsimd", f"{tag}_ex")
        ssca = p.sem(f"{tag}_sca")
        for ct in range(NT):
            if ct > 0:
                p.wait("gpsimd", f"{tag}_sca", 16 * ct)
            p.inc(f"{tag}_sca", 16)
            p.emit("gpsimd", lambda e, ct=ct: e.indirect_dma_start(
                dr["garrd"], IndirectOffsetOnAxis(iwork[:, ct:ct + 1], 0),
                ework[:, NT + ct:NT + ct + 1], None,
                bounds_check=K2 - 1, oob_is_err=False).then_inc(ssca, 16),
                ("inc", f"{tag}_sca", 16, "garr", ct))
        p.wait("sync", f"{tag}_sca")
        vv = p.dma("sync", wtile[:], dr["garrd"].rearrange("(p ct) one -> p (ct one)", ct=CT), f"{tag}_wrb")
        p.wait("vector", f"{tag}_wrb", vv)
        p.op("vector", lambda e: e.tensor_scalar(wtile[:], wtile[:], zsc[:, 5:6], None, ALU.mult), f"{tag}_wt")
        p.wait("sync", f"{tag}_wt")
        p.dma("sync", dr["wtile_o"], wtile[:], f"{tag}_bo")
        p.dma("sync", dr["sel_o"], sel[:], f"{tag}_bo")
        p.dma("sync", dr["ridx_o"], iwork[:, NT:2 * NT], f"{tag}_bo")
        p.dma("sync", dr["lnstat_o"], c["lnstat"][:, 0:2 * NT], f"{tag}_bo")
        p.wait("sync", f"{tag}_bo")


def emit_l0_attn_a_p2(p, nc, stack, dr, tag="a2"):
    K2 = 2048
    iwk = stack.enter_context(nc.sbuf_tensor(f"{tag}_iwk", [128, NT], I32))
    rec = stack.enter_context(nc.sbuf_tensor(f"{tag}_rec", [128, 4 * NT], F32))
    lns = stack.enter_context(nc.sbuf_tensor(f"{tag}_lns", [128, 2 * NT], F32))
    iot = stack.enter_context(nc.sbuf_tensor(f"{tag}_iot", [128, NT], F32))
    p.dma("sync", iwk[:], dr["ridx_i"], f"{tag}_cl")
    p.dma("sync", lns[:], dr["lnstat_i"], f"{tag}_cl")
    p.dma("sync", iot[:], dr["iota_cols"], f"{tag}_cl")
    p.wait("vector", f"{tag}_cl")
    recv = rec[:].rearrange("p (ct w) -> p w ct", w=4)
    lnv = lns[:].rearrange("p (t two) -> p two t", two=2)
    p.op("vector", lambda e: e.memset(rec[:], 0.0), f"{tag}_bd")
    p.wait("vector", f"{tag}_bd", p.cnt[f"{tag}_bd"])
    p.op("vector", lambda e: e.tensor_copy(recv[:, 0], iot[:]), f"{tag}_bd")
    p.op("vector", lambda e: e.tensor_copy(recv[:, 1], lnv[:, 0]), f"{tag}_bd")
    p.op("vector", lambda e: e.tensor_copy(recv[:, 2], lnv[:, 1]), f"{tag}_bd")
    p.wait("gpsimd", f"{tag}_bd")
    ssc = p.sem(f"{tag}_sc")
    for ct in range(NT):
        if ct > 0:
            p.wait("gpsimd", f"{tag}_sc", 16 * ct)
        p.inc(f"{tag}_sc", 16)
        p.emit("gpsimd", lambda e, ct=ct: e.indirect_dma_start(
            dr["idxd2"], IndirectOffsetOnAxis(iwk[:, ct:ct + 1], 0),
            rec[:, 4 * ct:4 * ct + 4], None,
            bounds_check=K2 - 1, oob_is_err=False).then_inc(ssc, 16),
            ("inc", f"{tag}_sc", 16, "rec", ct))
    p.wait("sync", f"{tag}_sc")
    p.wait("gpsimd", f"{tag}_sc")


def emit_l0_attn_a_p3(p, nc, stack, dr, tag="a0"):
    _a0_alias(p, tag)
    K2 = 2048
    CT = 16
    E = 8
    ones1 = stack.enter_context(nc.sbuf_tensor(f"{tag}_ones1", [1, 128], BF16))
    wtile = stack.enter_context(nc.sbuf_tensor(f"{tag}_wtile", [128, CT], F32))
    idxc = stack.enter_context(nc.sbuf_tensor(f"{tag}_idxc", [128, CT], I32))
    idxf = stack.enter_context(nc.sbuf_tensor(f"{tag}_idxf", [128, CT], F32))
    gates = stack.enter_context(nc.sbuf_tensor(f"{tag}_gates", [128, CT * E], F32))
    rc4 = stack.enter_context(nc.sbuf_tensor(f"{tag}_rc4", [128, 4 * CT], F32))
    murc = stack.enter_context(nc.sbuf_tensor(f"{tag}_murc", [128, 2 * CT], F32))
    acc = stack.enter_context(nc.sbuf_tensor(f"{tag}_acc", [128, CT * D], F32))
    n1T = stack.enter_context(nc.sbuf_tensor(f"{tag}_n1T", [128, DC * K2], BF16))
    st2 = stack.enter_context(nc.sbuf_tensor(f"{tag}_st2", [128, 4 * CT], F32))
    accv = acc[:].rearrange("p (ct d) -> p ct d", d=D)

    p.dma("sync", ones1[:], dr["ones1"], f"{tag}_rb")
    p.dma("sync", wtile[:], dr["wtile_i"], f"{tag}_rb")
    p.dma("sync", rc4[:], dr["idxd2"].rearrange("(p ct) w -> p (ct w)", ct=CT), f"{tag}_rb")
    p.wait("vector", f"{tag}_rb")
    p.wait("tensor", f"{tag}_rb")
    rcv = rc4[:].rearrange("p (ct w) -> p w ct", w=4)
    p.op("vector", lambda e: e.tensor_copy(idxf[:], rcv[:, 0]), f"{tag}_rb2")
    p.op("vector", lambda e: e.tensor_copy(murc[:, 0:CT], rcv[:, 1]), f"{tag}_rb2")
    p.op("vector", lambda e: e.reciprocal(murc[:, CT:2 * CT], rcv[:, 2]), f"{tag}_rb2")
    p.wait("vector", f"{tag}_rb2", p.cnt[f"{tag}_rb2"])
    p.op("vector", lambda e: e.tensor_copy(idxc[:], idxf[:]), f"{tag}_rb2")
    p.wait("gpsimd", f"{tag}_rb2")
    p.wait("vector", f"{tag}_rb2")

    def m_ln_block(ct2, gb, bb, dst_bf, statoff, src_ap):
        mcol = st2[:, statoff + 2 * ct2: statoff + 2 * ct2 + 1]
        rcol = st2[:, statoff + 2 * ct2 + 1: statoff + 2 * ct2 + 2]
        p.op("vector", lambda e: e.bn_stats(bnw[:, 0:6], src_ap[:, 0:512]))
        p.op("vector", lambda e: e.bn_stats(bnw[:, 6:12], src_ap[:, 512:1024]), f"{tag}_mbn")
        p.wait("vector", f"{tag}_mbn", p.cnt[f"{tag}_mbn"])
        p.op("vector", lambda e: e.bn_aggr(bnw[:, 12:14], bnw[:, 0:12]), f"{tag}_mbn")
        p.wait("vector", f"{tag}_mbn", p.cnt[f"{tag}_mbn"])
        p.op("vector", lambda e: e.tensor_copy(mcol, bnw[:, 12:13]))
        p.op("vector", lambda e: e.tensor_scalar(rcol, bnw[:, 13:14], 1e-5, None, ALU.add), f"{tag}_mbn")
        p.wait("scalar", f"{tag}_mbn", p.cnt[f"{tag}_mbn"])
        p.op("scalar", lambda e: e.activation(rcol, rcol, AF.Ln), f"{tag}_mrs")
        p.wait("scalar", f"{tag}_mrs", p.cnt[f"{tag}_mrs"])
        p.op("scalar", lambda e: e.activation(rcol, rcol, AF.Exp, scale=-0.5), f"{tag}_mrs")
        p.wait("vector", f"{tag}_mrs", p.cnt[f"{tag}_mrs"])
        p.op("vector", lambda e: e.tensor_scalar(tmf[:], src_ap, mcol, rcol, ALU.subtract, ALU.mult), f"{tag}_map")
        p.wait("vector", f"{tag}_map", p.cnt[f"{tag}_map"])
        p.op("vector", lambda e: e.tensor_tensor(tmf[:], tmf[:], gb[:], ALU.mult), f"{tag}_map")
        p.wait("vector", f"{tag}_map", p.cnt[f"{tag}_map"])
        p.op("vector", lambda e: e.tensor_tensor(dst_bf, tmf[:], bb[:], ALU.add), f"{tag}_map")

    with contextlib.ExitStack() as ps1:
        bc1 = ps1.enter_context(nc.sbuf_tensor(f"{tag}_bc1", [128, D], F32))
        bc2 = ps1.enter_context(nc.sbuf_tensor(f"{tag}_bc2", [128, D], F32))
        m_out_sb = ps1.enter_context(nc.sbuf_tensor(f"{tag}_mout", [128, 16 * D], BF16))
        uT = ps1.enter_context(nc.sbuf_tensor(f"{tag}_uT", [128, 16 * 256], BF16))
        silT = ps1.enter_context(nc.sbuf_tensor(f"{tag}_silT", [128, 16 * 256], BF16))
        winr = ps1.enter_context(nc.sbuf_tensor(f"{tag}_winr", [128, 2 * DC * 128], BF16))
        bnw = ps1.enter_context(nc.sbuf_tensor(f"{tag}_bnw", [128, 14], F32))
        tmf = ps1.enter_context(nc.sbuf_tensor(f"{tag}_tmf", [128, D], F32))
        nbf = ps1.enter_context(nc.sbuf_tensor(f"{tag}_nbf", [128, D], BF16))
        xpw = ps1.enter_context(nc.sbuf_tensor(f"{tag}_xpw", [128, 16 * 33], F32))
        xpa = ps1.enter_context(nc.sbuf_tensor(f"{tag}_xpa", [128, 4 * 16], F32))
        xpab = ps1.enter_context(nc.sbuf_tensor(f"{tag}_xpab", [128, 32], BF16))
        cv3 = ps1.enter_context(nc.sbuf_tensor(f"{tag}_cv3", [128, 32], F32))
        sb2 = ps1.enter_context(nc.sbuf_tensor(f"{tag}_sb2", [128, 512], BF16))
        srw = ps1.enter_context(nc.sbuf_tensor(f"{tag}_srw", [1, 2 * 512], F32))
        srwb = ps1.enter_context(nc.sbuf_tensor(f"{tag}_srwb", [1, 512], BF16))
        rows = ps1.enter_context(nc.sbuf_tensor(f"{tag}_rows", [1, 4 * D], BF16))
        ps_m = ps1.enter_context(nc.psum_tensor(f"{tag}_psM", [128, 2 * 512], F32))
        ps_w = ps1.enter_context(nc.psum_tensor(f"{tag}_psW", [1, 1024], F32))
        ps_c = ps1.enter_context(nc.psum_tensor(f"{tag}_psC", [128, 512], F32))
        ps_a = ps1.enter_context(nc.psum_tensor(f"{tag}_psA2", [128, 1024], F32))
        x0f = ps1.enter_context(nc.sbuf_tensor(f"{tag}_x0f", [128, 2 * D], F32))

        p.dma("gpsimd", rows[:, 0:D], dr["ln1g_row"], f"{tag}_mrw")
        p.dma("gpsimd", rows[:, D:2 * D], dr["ln1b_row"], f"{tag}_mrw")
        p.dma("gpsimd", rows[:, 2 * D:3 * D], dr["ln2g_row"], f"{tag}_mrw")
        p.dma("gpsimd", rows[:, 3 * D:4 * D], dr["ln2b_row"], f"{tag}_mrw")
        p.dma("sync", cv3[:, 0:16], dr["conv3_cols"], f"{tag}_mc")
        p.dma("sync", cv3[:, 16:32], dr["convb_cols"], f"{tag}_mc")
        p.dma("sync", xpw[:].rearrange("p (ch n) -> p ch n", n=33),
              dr["m_xproj"].rearrange("(ch p) n -> p ch n", p=128), f"{tag}_mc")
        p.dma("gpsimd", m_out_sb[:].rearrange("p (ch n) -> p ch n", n=D),
              dr["m_out_w"].rearrange("(ch p) n -> p ch n", p=128), f"{tag}_mrw")
        sga = p.sem(f"{tag}_ga")
        for ct2 in range(CT):
            p.inc(f"{tag}_ga", 16)
            p.emit("gpsimd", lambda e, ct2=ct2: e.indirect_dma_start(
                accv[:, ct2], None, dr["scrF"],
                IndirectOffsetOnAxis(idxc[:, ct2:ct2 + 1], 0)).then_inc(sga, 16),
                ("inc", f"{tag}_ga", 16))
        p.wait("vector", f"{tag}_ga")
        p.wait("vector", f"{tag}_mrw")
        p.wait("vector", f"{tag}_mc")
        p.wait("tensor", f"{tag}_mrw")
        p.wait("tensor", f"{tag}_mc")
        for i, dst in ((0, bc1), (1, bc2)):
            p.op("tensor", lambda e, i=i: e.matmul(ps_c[:], ones1[:], rows[:, i * D:i * D + 512], start=True, stop=True), f"{tag}_mb")
            p.wait("vector", f"{tag}_mb", p.cnt[f"{tag}_mb"])
            p.op("vector", lambda e, dst=dst: e.tensor_copy(dst[:, 0:512], ps_c[:]), f"{tag}_mbe")
            p.wait("tensor", f"{tag}_mbe", p.cnt[f"{tag}_mbe"])
            p.op("tensor", lambda e, i=i: e.matmul(ps_c[:], ones1[:], rows[:, i * D + 512:(i + 1) * D], start=True, stop=True), f"{tag}_mb")
            p.wait("vector", f"{tag}_mb", p.cnt[f"{tag}_mb"])
            p.op("vector", lambda e, dst=dst: e.tensor_copy(dst[:, 512:1024], ps_c[:]), f"{tag}_mbe")
            p.wait("tensor", f"{tag}_mbe", p.cnt[f"{tag}_mbe"])
        for ct2 in range(CT):
            rr2 = ct2 % 2
            s_x0 = f"{tag}_x0s{rr2}"
            if ct2 >= 2:
                p.wait("vector", s_x0, p.cnt[s_x0])
            p.op("vector", lambda e, ct2=ct2, rr2=rr2: e.tensor_scalar(
                x0f[:, rr2 * D:(rr2 + 1) * D], accv[:, ct2],
                murc[:, CT + ct2:CT + ct2 + 1], murc[:, ct2:ct2 + 1],
                ALU.mult, ALU.add), f"{tag}_x0v")
            p.wait("sync", f"{tag}_x0v", p.cnt[f"{tag}_x0v"])
            p.dma("sync", dr["x0c"].rearrange("(t p) d -> t p d", p=128)[ct2],
                  x0f[:, rr2 * D:(rr2 + 1) * D], s_x0)
            if ct2 >= 1:
                p.wait("vector", f"{tag}_n1s0", p.cnt[f"{tag}_n1s0"])
            m_ln_block(ct2, bc1, bc2, nbf[:], 0, accv[:, ct2])
            p.wait("sync", f"{tag}_map", p.cnt[f"{tag}_map"])
            p.dma("sync", dr["scrN"].rearrange("(t p) d -> t p d", p=128)[ct2],
                  nbf[:], f"{tag}_n1s0")
        p.wait("sync", f"{tag}_n1s0")
        for ch in range(DC):
            p.dmat("sync", n1T[:, ch * K2:(ch + 1) * K2],
                   dr["scrN"][:, ch * 128:(ch + 1) * 128], f"{tag}_n1t")
        p.op("vector", lambda e: e.tensor_reduce(
            xpa[:, 0:16], xpw[:].rearrange("p (ch n) -> p ch n", n=33)[:, :, 16:32],
            AX.X, ALU.add), f"{tag}_xp")
        p.op("vector", lambda e: e.tensor_copy(
            xpa[:, 16:32], xpw[:].rearrange("p (ch n) -> p ch n", n=33)[:, :, 32]), f"{tag}_xp")
        p.wait("vector", f"{tag}_xp", p.cnt[f"{tag}_xp"])
        p.op("vector", lambda e: e.tensor_copy(xpab[:, 0:16], xpa[:, 0:16]), f"{tag}_xp")
        p.op("vector", lambda e: e.tensor_copy(xpab[:, 16:32], xpa[:, 16:32]), f"{tag}_xp")
        p.wait("tensor", f"{tag}_n1t")
        p.wait("tensor", f"{tag}_xp")
        TGW = 256
        for tg in range(K2 // TGW):
            if tg >= 1:
                p.wait("vector", f"{tag}_aev", 2 * tg)
                p.wait("scalar", f"{tag}_aev", 2 * tg)
            for ch in range(32):
                rw = ch % 2
                s_wl = f"{tag}_mwl{rw}"
                wv = winr[:, rw * DC * 128:(rw + 1) * DC * 128]
                if ch >= 2 or tg > 0:
                    p.wait("gpsimd", f"{tag}_mmm", max(0, p.cnt.get(f"{tag}_mmm", 0) - 1))
                p.dma("gpsimd", wv.rearrange("p (c n) -> p c n", n=128),
                      dr["m_in_w"].rearrange("(c p) (ch n) -> p c ch n", p=128, n=128)[:, :, ch], s_wl)
                p.wait("tensor", s_wl, p.cnt[s_wl])
                rb = ch % 2
                pm = ps_m[:, rb * 512:rb * 512 + TGW]
                if p.cnt.get(f"{tag}_mev", 0) >= 2:
                    p.wait("tensor", f"{tag}_mev", p.cnt[f"{tag}_mev"] - 1)
                for j in range(DC):
                    p.op("tensor", lambda e, j=j, wv=wv, pm=pm, tg=tg: e.matmul(
                        pm, wv[:, j * 128:(j + 1) * 128],
                        n1T[:, j * K2 + tg * TGW: j * K2 + (tg + 1) * TGW],
                        start=(j == 0), stop=(j == DC - 1)),
                        f"{tag}_mmm" if j == DC - 1 else None)
                if ch < 16:
                    p.wait("vector", f"{tag}_mmm", p.cnt[f"{tag}_mmm"])
                    p.op("vector", lambda e, ch=ch, pm=pm: e.tensor_scalar(
                        uT[:, ch * TGW:(ch + 1) * TGW], pm,
                        cv3[:, ch:ch + 1], cv3[:, 16 + ch:17 + ch], ALU.mult, ALU.add), f"{tag}_mev")
                else:
                    p.wait("scalar", f"{tag}_mmm", p.cnt[f"{tag}_mmm"])
                    p.op("scalar", lambda e, ch=ch, pm=pm: e.activation(
                        silT[:, (ch - 16) * TGW:(ch - 15) * TGW], pm, AF.Silu), f"{tag}_mev")
            p.wait("tensor", f"{tag}_mev")
            for ch in range(16):
                p.op("tensor", lambda e, ch=ch: e.matmul(
                    ps_w[0:1, 0:TGW], xpab[:, ch:ch + 1], uT[:, ch * TGW:(ch + 1) * TGW],
                    start=(ch == 0), stop=(ch == 15)), f"{tag}_bcm" if ch == 15 else None)
            for ch in range(16):
                p.op("tensor", lambda e, ch=ch: e.matmul(
                    ps_w[0:1, 512:512 + TGW], xpab[:, 16 + ch:17 + ch], uT[:, ch * TGW:(ch + 1) * TGW],
                    start=(ch == 0), stop=(ch == 15)), f"{tag}_bcm" if ch == 15 else None)
            p.wait("vector", f"{tag}_bcm", p.cnt[f"{tag}_bcm"])
            p.op("vector", lambda e: e.tensor_copy(srw[0:1, 512:512 + TGW], ps_w[0:1, 512:512 + TGW]), f"{tag}_srw")
            p.wait("vector", f"{tag}_srw", p.cnt[f"{tag}_srw"])
            p.op("vector", lambda e: e.tensor_tensor(srw[0:1, 0:TGW], ps_w[0:1, 0:TGW], srw[0:1, 512:512 + TGW], ALU.mult), f"{tag}_srw")
            p.wait("vector", f"{tag}_srw", p.cnt[f"{tag}_srw"])
            p.op("vector", lambda e: e.tensor_copy(srwb[:, 0:TGW], srw[0:1, 0:TGW]), f"{tag}_srw")
            p.wait("tensor", f"{tag}_srw", p.cnt[f"{tag}_srw"])
            p.op("tensor", lambda e: e.matmul(ps_c[:, 0:TGW], ones1[:], srwb[:, 0:TGW], start=True, stop=True), f"{tag}_sbm")
            p.wait("vector", f"{tag}_sbm", p.cnt[f"{tag}_sbm"])
            p.op("vector", lambda e: e.tensor_copy(sb2[:, 0:TGW], ps_c[:, 0:TGW]), f"{tag}_sbe")
            p.wait("vector", f"{tag}_sbe", p.cnt[f"{tag}_sbe"])
            for ch in range(16):
                p.op("vector", lambda e, ch=ch: e.tensor_tensor(
                    uT[:, ch * TGW:(ch + 1) * TGW], uT[:, ch * TGW:(ch + 1) * TGW], sb2[:, 0:TGW], ALU.mult), None)
            p.op("vector", lambda e: e.engine_nop(), f"{tag}_ym")
            p.wait("vector", f"{tag}_ym", p.cnt[f"{tag}_ym"])
            for ch in range(16):
                p.op("vector", lambda e, ch=ch: e.tensor_tensor(
                    uT[:, ch * TGW:(ch + 1) * TGW], uT[:, ch * TGW:(ch + 1) * TGW],
                    silT[:, ch * TGW:(ch + 1) * TGW], ALU.mult),
                    f"{tag}_ym2" if ch == 15 else None)
            p.wait("tensor", f"{tag}_ym2", p.cnt[f"{tag}_ym2"])
            for ts4 in range(TGW // 128):
                ct2 = tg * (TGW // 128) + ts4
                if p.cnt.get(f"{tag}_aev", 0) >= 1:
                    p.wait("tensor", f"{tag}_aev", p.cnt[f"{tag}_aev"])
                for half in range(2):
                    for hc in range(16):
                        p.op("tensor", lambda e, hc=hc, ts4=ts4, half=half: e.matmul(
                            ps_a[:, half * 512:(half + 1) * 512],
                            uT[:, hc * TGW + ts4 * 128: hc * TGW + (ts4 + 1) * 128],
                            m_out_sb[:, hc * D + half * 512: hc * D + (half + 1) * 512],
                            start=(hc == 0), stop=(hc == 15)),
                            f"{tag}_amm" if (hc == 15 and half == 1) else None)
                p.wait("vector", f"{tag}_amm", p.cnt[f"{tag}_amm"])
                p.op("vector", lambda e, ct2=ct2: e.tensor_tensor(
                    accv[:, ct2], ps_a[:], accv[:, ct2], ALU.add), f"{tag}_aev")
        p.wait("vector", f"{tag}_aev")
        for i, dst in ((2, bc1), (3, bc2)):
            p.op("tensor", lambda e, i=i: e.matmul(ps_c[:], ones1[:], rows[:, i * D:i * D + 512], start=True, stop=True), f"{tag}_mb")
            p.wait("vector", f"{tag}_mb", p.cnt[f"{tag}_mb"])
            p.op("vector", lambda e, dst=dst: e.tensor_copy(dst[:, 0:512], ps_c[:]), f"{tag}_mbe")
            p.wait("tensor", f"{tag}_mbe", p.cnt[f"{tag}_mbe"])
            p.op("tensor", lambda e, i=i: e.matmul(ps_c[:], ones1[:], rows[:, i * D + 512:(i + 1) * D], start=True, stop=True), f"{tag}_mb")
            p.wait("vector", f"{tag}_mb", p.cnt[f"{tag}_mb"])
            p.op("vector", lambda e, dst=dst: e.tensor_copy(dst[:, 512:1024], ps_c[:]), f"{tag}_mbe")
            p.wait("tensor", f"{tag}_mbe", p.cnt[f"{tag}_mbe"])
        for ct2 in range(CT):
            if ct2 >= 1:
                p.wait("vector", f"{tag}_n2s0", p.cnt[f"{tag}_n2s0"])
            m_ln_block(ct2, bc1, bc2, nbf[:], 2 * CT, accv[:, ct2])
            p.wait("sync", f"{tag}_map", p.cnt[f"{tag}_map"])
            p.dma("sync", dr["scrN"].rearrange("(t p) d -> t p d", p=128)[ct2],
                  nbf[:], f"{tag}_n2s0")
        p.wait("sync", f"{tag}_n2s0")
        for ch in range(DC):
            p.dmat("sync", n1T[:, ch * K2:(ch + 1) * K2],
                   dr["scrN"][:, ch * 128:(ch + 1) * 128], f"{tag}_n2t")
        p.wait("vector", f"{tag}_n2t")
        p.wait("tensor", f"{tag}_n2t")
        p.wait("gpsimd", f"{tag}_n2t")
        p.wait("scalar", f"{tag}_n2t")
        p.wait("vector", f"{tag}_x0s0")
        p.wait("vector", f"{tag}_x0s1")
        p.op("vector", lambda e: e.engine_nop(), f"{tag}_Mdone")

    with contextlib.ExitStack() as ps2:
        W1h = ps2.enter_context(nc.sbuf_tensor(f"{tag}_W1h", [128, DC * 2048], BF16))
        W2h = ps2.enter_context(nc.sbuf_tensor(f"{tag}_W2h", [128, 16 * D], BF16))
        h1b = ps2.enter_context(nc.sbuf_tensor(f"{tag}_h1b", [128, 16 * 512], BF16))
        b1c2 = ps2.enter_context(nc.sbuf_tensor(f"{tag}_b1c2", [128, E * 32], F32))
        b2sb = ps2.enter_context(nc.sbuf_tensor(f"{tag}_b2sb", [8, D], BF16))
        gTt = ps2.enter_context(nc.sbuf_tensor(f"{tag}_gTt", [8, 128], BF16))
        identb2 = ps2.enter_context(nc.sbuf_tensor(f"{tag}_identb2", [128, 128], BF16))
        mrw = ps2.enter_context(nc.sbuf_tensor(f"{tag}_mrw", [128, DC * E], BF16))
        mrb = ps2.enter_context(nc.sbuf_tensor(f"{tag}_mrb", [1, E], BF16))
        gw = ps2.enter_context(nc.sbuf_tensor(f"{tag}_gw", [128, 5 * E], F32))
        zout = ps2.enter_context(nc.sbuf_tensor(f"{tag}_zout", [1, D], BF16))
        ps_g = ps2.enter_context(nc.psum_tensor(f"{tag}_psG", [128, 512], F32))
        ps_gb = ps2.enter_context(nc.psum_tensor(f"{tag}_psGB", [128, 128], BF16))
        ps_h1 = ps2.enter_context(nc.psum_tensor(f"{tag}_psH1", [128, 2 * 512], F32))
        ps_eo = ps2.enter_context(nc.psum_tensor(f"{tag}_psEO", [128, 2 * 1024], F32))
        for eng in ENGS:
            p.wait(eng, f"{tag}_Mdone")
        p.dma("gpsimd", mrw[:].rearrange("p (ch n) -> p ch n", n=E),
              dr["moe_r_w"].rearrange("(ch p) n -> p ch n", p=128), f"{tag}_gl")
        p.dma("gpsimd", mrb[:], dr["moe_r_b_row"], f"{tag}_gl")
        p.dma("sync", b1c2[:], dr["moe_b1_cols"], f"{tag}_gl2")
        p.dma("gpsimd", b2sb[:], dr["moe_b2"], f"{tag}_gl")
        p.dma("sync", identb2[:], dr["identb"], f"{tag}_gl2")
        p.wait("tensor", f"{tag}_gl")
        p.wait("tensor", f"{tag}_gl2")
        p.wait("vector", f"{tag}_gl")
        p.op("vector", lambda e: e.memset(zout[:], 0.0), f"{tag}_zo")
        p.wait("tensor", f"{tag}_zo")
        for ct2 in range(CT):
            if ct2 >= 1:
                p.wait("tensor", f"{tag}_gev", 4 * ct2)
            for ch in range(DC):
                p.op("tensor", lambda e, ch=ch, ct2=ct2: e.matmul(
                    ps_g[:, 0:E], n1T[:, ch * K2 + ct2 * 128: ch * K2 + (ct2 + 1) * 128],
                    mrw[:, ch * E:(ch + 1) * E], start=(ch == 0), stop=False), None)
            p.op("tensor", lambda e: e.matmul(
                ps_g[:, 0:E], ones1[:], mrb[:], start=False, stop=True), f"{tag}_gmm")
            p.wait("vector", f"{tag}_gmm", p.cnt[f"{tag}_gmm"])
            p.op("vector", lambda e: e.tensor_reduce(gw[:, 0:1], ps_g[:, 0:E], AX.X, ALU.max), f"{tag}_gev")
            p.wait("vector", f"{tag}_gev", p.cnt[f"{tag}_gev"])
            p.op("vector", lambda e: e.tensor_scalar(gw[:, 1:2], gw[:, 0:1], -1.0, None, ALU.mult), f"{tag}_gev")
            p.wait("scalar", f"{tag}_gev", p.cnt[f"{tag}_gev"])
            p.op("scalar", lambda e: e.activation(gw[:, 8:16], ps_g[:, 0:E], AF.Exp, bias=gw[:, 1:2]), f"{tag}_gev")
            p.wait("vector", f"{tag}_gev", p.cnt[f"{tag}_gev"])
            p.op("vector", lambda e: e.tensor_reduce(gw[:, 2:3], gw[:, 8:16], AX.X, ALU.add), f"{tag}_gpr")
            p.wait("vector", f"{tag}_gpr", p.cnt[f"{tag}_gpr"])
            p.op("vector", lambda e: e.reciprocal(gw[:, 3:4], gw[:, 2:3]), f"{tag}_gpr")
            p.wait("vector", f"{tag}_gpr", p.cnt[f"{tag}_gpr"])
            p.op("vector", lambda e: e.tensor_scalar(gw[:, 8:16], gw[:, 8:16], gw[:, 3:4], None, ALU.mult), f"{tag}_gpr")
            p.wait("vector", f"{tag}_gpr", p.cnt[f"{tag}_gpr"])
            for eL in range(E):
                p.op("vector", lambda e, eL=eL: e.tensor_scalar(
                    gw[:, 16:24], gw[:, 8:16], gw[:, 8 + eL:9 + eL], None, ALU.is_gt), f"{tag}_gt")
                p.wait("vector", f"{tag}_gt", p.cnt[f"{tag}_gt"])
                p.op("vector", lambda e, eL=eL: e.tensor_reduce(
                    gw[:, 24 + eL:25 + eL], gw[:, 16:24], AX.X, ALU.add), f"{tag}_gt")
                p.wait("vector", f"{tag}_gt", p.cnt[f"{tag}_gt"])
            p.op("vector", lambda e: e.tensor_scalar(gw[:, 24:32], gw[:, 24:32], 2.0, None, ALU.is_lt), f"{tag}_gt")
            p.wait("vector", f"{tag}_gt", p.cnt[f"{tag}_gt"])
            p.op("vector", lambda e: e.tensor_tensor(gw[:, 8:16], gw[:, 8:16], gw[:, 24:32], ALU.mult), f"{tag}_gt")
            p.wait("vector", f"{tag}_gt", p.cnt[f"{tag}_gt"])
            p.op("vector", lambda e: e.tensor_reduce(gw[:, 4:5], gw[:, 8:16], AX.X, ALU.add), f"{tag}_gt")
            p.wait("vector", f"{tag}_gt", p.cnt[f"{tag}_gt"])
            p.op("vector", lambda e: e.reciprocal(gw[:, 5:6], gw[:, 4:5]), f"{tag}_gt")
            p.wait("vector", f"{tag}_gt", p.cnt[f"{tag}_gt"])
            p.op("vector", lambda e, ct2=ct2: e.tensor_scalar(
                gates[:, ct2 * E:(ct2 + 1) * E], gw[:, 8:16], gw[:, 5:6], None, ALU.mult), f"{tag}_gev")
        p.wait("gpsimd", f"{tag}_gev")
        for eX in range(E):
            for half in range(2):
                hw = eX * 2 + half
                rw = hw % 2
                s_w1 = f"{tag}_w1l{rw}"
                s_w2 = f"{tag}_w2l{rw}"
                if hw >= 1:
                    p.wait("gpsimd", f"{tag}_eev", 16 * hw)
                p.dma("gpsimd", W1h[:].rearrange("p (ch n) -> p ch n", n=2048),
                      dr["moe_w1"][eX].rearrange("(ch p) (hh n) -> p ch hh n", p=128, hh=2)[:, :, half], s_w1)
                p.dma("gpsimd", W2h[:].rearrange("p (hc n) -> p hc n", n=D),
                      dr["moe_w2"][eX].rearrange("(hh hc p) n -> p hh hc n", p=128, hh=2)[:, half], s_w2)
                p.wait("tensor", s_w1, p.cnt[s_w1])
                p.wait("tensor", s_w2, p.cnt[s_w2])
                for tg in range(4):
                    if p.cnt.get(f"{tag}_eom", 0) > 0:
                        p.wait("scalar", f"{tag}_eom", p.cnt[f"{tag}_eom"])
                    for hc in range(16):
                        rb = hc % 2
                        ph = ps_h1[:, rb * 512:(rb + 1) * 512]
                        if p.cnt.get(f"{tag}_h1e", 0) >= 2:
                            p.wait("tensor", f"{tag}_h1e", p.cnt[f"{tag}_h1e"] - 1)
                        for ch in range(DC):
                            p.op("tensor", lambda e, ch=ch, hc=hc, tg=tg, ph=ph: e.matmul(
                                ph, W1h[:, ch * 2048 + hc * 128: ch * 2048 + (hc + 1) * 128],
                                n1T[:, ch * K2 + tg * 512: ch * K2 + (tg + 1) * 512],
                                start=(ch == 0), stop=(ch == DC - 1)),
                                f"{tag}_h1m" if ch == DC - 1 else None)
                        p.wait("scalar", f"{tag}_h1m", p.cnt[f"{tag}_h1m"])
                        p.op("scalar", lambda e, hc=hc, eX=eX, half=half, ph=ph: e.activation(
                            h1b[:, hc * 512:(hc + 1) * 512], ph, AF.Gelu,
                            bias=b1c2[:, eX * 32 + half * 16 + hc:eX * 32 + half * 16 + hc + 1]),
                            f"{tag}_h1e")
                    p.wait("tensor", f"{tag}_h1e", p.cnt[f"{tag}_h1e"])
                    for ts4 in range(4):
                        ct2 = tg * 4 + ts4
                        rb = ct2 % 2
                        pe = ps_eo[:, rb * 1024:(rb + 1) * 1024]
                        if p.cnt.get(f"{tag}_eev", 0) >= 2:
                            p.wait("tensor", f"{tag}_eev", p.cnt[f"{tag}_eev"] - 1)
                        for hf in range(2):
                            for hc in range(16):
                                p.op("tensor", lambda e, hc=hc, ts4=ts4, hf=hf, pe=pe, half=half: e.matmul(
                                    pe[:, hf * 512:(hf + 1) * 512],
                                    h1b[:, hc * 512 + ts4 * 128: hc * 512 + (ts4 + 1) * 128],
                                    W2h[:, hc * D + hf * 512: hc * D + (hf + 1) * 512],
                                    start=(hc == 0), stop=(half == 1 and hc == 15)),
                                    f"{tag}_eom" if (half == 1 and hf == 1 and hc == 15) else None)
                            if half == 0:
                                p.op("tensor", lambda e, hf=hf, pe=pe: e.matmul(
                                    pe[:, hf * 512:(hf + 1) * 512], ones1[:],
                                    zout[:, hf * 512:(hf + 1) * 512],
                                    start=False, stop=True),
                                    f"{tag}_eom" if hf == 1 else None)
                        p.wait("vector", f"{tag}_eom", p.cnt[f"{tag}_eom"])
                        p.op("vector", lambda e, ct2=ct2, pe=pe, eX=eX: e.scalar_tensor_tensor(
                            accv[:, ct2], pe, gates[:, ct2 * E + eX:ct2 * E + eX + 1],
                            accv[:, ct2], ALU.mult, ALU.add), f"{tag}_eev")
        p.wait("tensor", f"{tag}_eev")
        for ct2 in range(CT):
            p.op("vector", lambda e, ct2=ct2: e.tensor_copy(
                gw[:, 8:16].bitcast(BF16)[:, 0:8], gates[:, ct2 * E:(ct2 + 1) * E]), f"{tag}_gb")
            p.wait("tensor", f"{tag}_gb", p.cnt[f"{tag}_gb"])
            p.op("tensor", lambda e: e.transpose(
                ps_g[:, 0:128].bitcast(BF16)[0:8, 0:128] if False else ps_gb[0:8, 0:128],
                gw[:, 8:16].bitcast(BF16)[:, 0:8], identb2[:]), f"{tag}_gb2")
            p.wait("vector", f"{tag}_gb2", p.cnt[f"{tag}_gb2"])
            p.op("vector", lambda e: e.tensor_copy(gTt[:], ps_gb[0:8, 0:128]), f"{tag}_gb3")
            p.wait("tensor", f"{tag}_gb3", p.cnt[f"{tag}_gb3"])
            if ct2 >= 1:
                p.wait("tensor", f"{tag}_bev2", ct2)
            for hf in range(2):
                p.op("tensor", lambda e, hf=hf: e.matmul(
                    ps_eo[:, hf * 512:(hf + 1) * 512], gTt[:],
                    b2sb[:, hf * 512:(hf + 1) * 512], start=True, stop=True),
                    f"{tag}_bmm2" if hf == 1 else None)
            p.wait("vector", f"{tag}_bmm2", p.cnt[f"{tag}_bmm2"])
            p.op("vector", lambda e, ct2=ct2: e.tensor_tensor(
                accv[:, ct2], ps_eo[:, 0:1024], accv[:, ct2], ALU.add), f"{tag}_bev2")
        p.wait("vector", f"{tag}_eev")
        p.wait("vector", f"{tag}_bev2")
        p.wait("sync", f"{tag}_bev2")
        p.wait("gpsimd", f"{tag}_bev2")
        p.wait("tensor", f"{tag}_bev2")
        p.op("vector", lambda e: e.engine_nop(), f"{tag}_Edone")

    with contextlib.ExitStack() as ps3:
        x0g = ps3.enter_context(nc.sbuf_tensor(f"{tag}_x0g", [128, 2 * D], F32))
        xb0 = ps3.enter_context(nc.sbuf_tensor(f"{tag}_xb0", [128, 2 * D], F32))
        xb1 = ps3.enter_context(nc.sbuf_tensor(f"{tag}_xb1", [128, 2 * D], F32))
        outr = ps3.enter_context(nc.sbuf_tensor(f"{tag}_outr", [128, 2 * D], F32))
        for eng in ENGS:
            p.wait(eng, f"{tag}_Edone")
        xiv = dr["x_in"].rearrange("(t p) d -> t p d", p=128)
        xfv = dr["scrF"].rearrange("(t p) d -> t p d", p=128)
        xov = dr["x_out"].rearrange("(t p) d -> t p d", p=128)
        for t in range(NT):
            rr = t % 2
            s_b0 = f"{tag}_fb0{rr}"
            s_b1 = f"{tag}_fb1{rr}"
            s_bo = f"{tag}_fbo{rr}"
            if t >= 2:
                p.wait("sync", f"{tag}_fev", t - 1)
                p.wait("sync", s_bo, p.cnt[s_bo])
            v0 = p.dma("sync", xb0[:, rr * D:(rr + 1) * D], xiv[t], s_b0)
            v1 = p.dma("sync", xb1[:, rr * D:(rr + 1) * D], xfv[t], s_b1)
            p.wait("vector", s_b0, v0)
            p.wait("vector", s_b1, v1)
            if t >= 2:
                p.wait("vector", s_bo, p.cnt[s_bo])
            p.op("vector", lambda e, rr=rr: e.tensor_tensor(
                xb0[:, rr * D:(rr + 1) * D], xb0[:, rr * D:(rr + 1) * D],
                xb1[:, rr * D:(rr + 1) * D], ALU.add), f"{tag}_fev")
            p.wait("sync", f"{tag}_fev", p.cnt[f"{tag}_fev"])
            p.dma("sync", xov[t], xb0[:, rr * D:(rr + 1) * D], s_bo)
        for rr in range(2):
            p.wait("sync", f"{tag}_fbo{rr}")
        p.wait("gpsimd", f"{tag}_fbo0")
        p.wait("gpsimd", f"{tag}_fbo1")
        p.wait("sync", f"{tag}_x0s0")
        p.wait("sync", f"{tag}_x0s1")
        x0cv = dr["x0c"].rearrange("(t p) d -> t p d", p=128)
        sfs = p.sem(f"{tag}_fsc")
        for ct2 in range(CT):
            rr = ct2 % 2
            if ct2 >= 1:
                p.wait("gpsimd", f"{tag}_fsc", p.cnt[f"{tag}_fsc"])
            if ct2 >= 2:
                p.wait("gpsimd", f"{tag}_fw2", p.cnt[f"{tag}_fw2"])
                p.wait("vector", f"{tag}_fsc", p.cnt[f"{tag}_fsc"])
                p.wait("sync", f"{tag}_fw2", p.cnt[f"{tag}_fw2"])
            s_gx = f"{tag}_gx{rr}"
            v0 = p.dma("sync", x0g[:, rr * D:(rr + 1) * D], x0cv[ct2], s_gx)
            p.wait("vector", s_gx, v0)
            p.op("vector", lambda e, ct2=ct2, rr=rr: e.tensor_scalar(
                outr[:, rr * D:(rr + 1) * D], accv[:, ct2], wtile[:, ct2:ct2 + 1],
                None, ALU.mult), f"{tag}_fw")
            p.wait("vector", f"{tag}_fw", p.cnt[f"{tag}_fw"])
            p.op("vector", lambda e, rr=rr: e.tensor_tensor(
                outr[:, rr * D:(rr + 1) * D], outr[:, rr * D:(rr + 1) * D],
                x0g[:, rr * D:(rr + 1) * D], ALU.add), f"{tag}_fw2")
            p.wait("gpsimd", f"{tag}_fw2", p.cnt[f"{tag}_fw2"])
            p.inc(f"{tag}_fsc", 16)
            p.emit("gpsimd", lambda e, ct2=ct2, rr=rr: e.indirect_dma_start(
                dr["x_out"], IndirectOffsetOnAxis(idxc[:, ct2:ct2 + 1], 0),
                outr[:, rr * D:(rr + 1) * D], None).then_inc(sfs, 16),
                ("inc", f"{tag}_fsc", 16))
        p.wait("gpsimd", f"{tag}_fsc")
        p.wait("sync", f"{tag}_fsc")




def a0_consts():
    cst = (np.arange(128)[:, None] > np.arange(128)[None, :]).astype(np.float32).T
    iota_cols = np.empty((128, NT), np.int32)
    for t in range(NT):
        iota_cols[:, t] = t * 128 + np.arange(128)
    return cst, iota_cols


def build_l0_attn_a_p1():
    nc = bass.Bass("TRN2", target_bir_lowering=False)
    TEN = lambda n, sh, dt=F32, kind="ExternalInput": nc.dram_tensor(n, sh, dt, kind=kind).ap()
    x_in = TEN("x_in", [T_FULL, D])
    modr_w1 = TEN("modr_w1", [D, 256])
    modr_b1_cols = TEN("modr_b1_cols", [128, 2])
    modr_w2_cols = TEN("modr_w2_cols", [128, 2])
    modr_b2 = TEN("modr_b2", [1, 1])
    onesf = TEN("onesf", [1, 128])
    onescf = TEN("onescf", [128, 1])
    one1f = TEN("one1f", [1, 1])
    identf = TEN("identf", [128, 128])
    cstl = TEN("cstl", [128, 128])
    scr = TEN("scr", [T_FULL, D], BF16, "Internal")
    garrd = TEN("garrd", [2048, 1], F32, "Internal")
    scrF = TEN("scrF", [T_FULL, D], F32, "ExternalOutput")
    wtile_o = TEN("wtile_o", [128, 16], F32, "ExternalOutput")
    sel_o = TEN("sel_o", [128, NT], F32, "ExternalOutput")
    ridx_o = TEN("ridx_o", [128, NT], I32, "ExternalOutput")
    lnstat_o = TEN("lnstat_o", [128, 2 * NT], F32, "ExternalOutput")
    with contextlib.ExitStack() as stack:
        p = Prog(nc, stack)
        dr = dict(x_in=x_in, scr=scr, scrF=scrF, garrd=garrd,
                  modr_w1=modr_w1, modr_b1_cols=modr_b1_cols,
                  modr_w2_cols=modr_w2_cols, modr_b2=modr_b2,
                  onesf=onesf, onescf=onescf, one1f=one1f, identf=identf,
                  cstl=cstl, wtile_o=wtile_o, sel_o=sel_o, ridx_o=ridx_o,
                  lnstat_o=lnstat_o)
        emit_l0_attn_a_p1(p, nc, stack, dr)
        p.check_deadlock()
        p.realize()
    return nc


def build_l0_attn_a_p2():
    nc = bass.Bass("TRN2", target_bir_lowering=False)
    TEN = lambda n, sh, dt=F32, kind="ExternalInput": nc.dram_tensor(n, sh, dt, kind=kind).ap()
    ridx_i = TEN("ridx_i", [128, NT], I32)
    lnstat_i = TEN("lnstat_i", [128, 2 * NT])
    iota_cols = TEN("iota_cols", [128, NT])
    idxd2 = TEN("idxd2", [2048, 4], F32, "ExternalOutput")
    with contextlib.ExitStack() as stack:
        p = Prog(nc, stack)
        dr = dict(ridx_i=ridx_i, lnstat_i=lnstat_i, iota_cols=iota_cols,
                  idxd2=idxd2)
        emit_l0_attn_a_p2(p, nc, stack, dr)
        p.check_deadlock()
        p.realize()
    return nc


def build_l0_attn_a_p3():
    nc = bass.Bass("TRN2", target_bir_lowering=False)
    TEN = lambda n, sh, dt=F32, kind="ExternalInput": nc.dram_tensor(n, sh, dt, kind=kind).ap()
    x_in = TEN("x_in", [T_FULL, D])
    scrF = TEN("scrF", [T_FULL, D])
    wtile_i = TEN("wtile_i", [128, 16])
    idxd2 = TEN("idxd2", [2048, 4])
    ln1g_row = TEN("ln1g_row", [1, D])
    ln1b_row = TEN("ln1b_row", [1, D])
    ln2g_row = TEN("ln2g_row", [1, D])
    ln2b_row = TEN("ln2b_row", [1, D])
    m_in_w = TEN("m_in_w", [D, 4096])
    conv3_cols = TEN("conv3_cols", [128, 16])
    convb_cols = TEN("convb_cols", [128, 16])
    m_xproj = TEN("m_xproj", [2048, 33])
    m_out_w = TEN("m_out_w", [2048, D])
    moe_r_w = TEN("moe_r_w", [D, 8])
    moe_r_b_row = TEN("moe_r_b_row", [1, 8])
    moe_w1 = TEN("moe_w1", [8, D, 4096])
    moe_b1_cols = TEN("moe_b1_cols", [128, 256])
    moe_w2 = TEN("moe_w2", [8, 4096, D])
    moe_b2 = TEN("moe_b2", [8, D])
    identb = TEN("identb", [128, 128], BF16)
    ones1 = TEN("ones1", [1, 128], BF16)
    x_out = TEN("x_out", [T_FULL, D], F32, "ExternalOutput")
    scrN = TEN("scrN", [2048, D], BF16, "Internal")
    x0c = TEN("x0c", [2048, D], F32, "Internal")
    with contextlib.ExitStack() as stack:
        p = Prog(nc, stack)
        dr = dict(x_in=x_in, x_out=x_out, scrF=scrF, scrN=scrN, x0c=x0c,
                  wtile_i=wtile_i, idxd2=idxd2,
                  ln1g_row=ln1g_row, ln1b_row=ln1b_row,
                  ln2g_row=ln2g_row, ln2b_row=ln2b_row, m_in_w=m_in_w,
                  conv3_cols=conv3_cols, convb_cols=convb_cols,
                  m_xproj=m_xproj, m_out_w=m_out_w, moe_r_w=moe_r_w,
                  moe_r_b_row=moe_r_b_row, moe_w1=moe_w1,
                  moe_b1_cols=moe_b1_cols, moe_w2=moe_w2,
                  moe_b2=moe_b2, identb=identb, ones1=ones1)
        emit_l0_attn_a_p3(p, nc, stack, dr)
        p.check_deadlock()
        p.realize()
    return nc


def a0_host_inputs(g):
    cst, iota_cols = a0_consts()
    d1 = dict(
        modr_w1=g["modr_w1"],
        modr_b1_cols=np.ascontiguousarray(g["modr_b1"].reshape(2, 128).T),
        modr_w2_cols=np.ascontiguousarray(g["modr_w2"].reshape(2, 128).T),
        modr_b2=g["modr_b2"].reshape(1, 1),
        onesf=np.ones((1, 128), np.float32),
        onescf=np.ones((128, 1), np.float32),
        one1f=np.ones((1, 1), np.float32),
        identf=np.eye(128, dtype=np.float32),
        cstl=cst,
    )
    d2 = dict(iota_cols=iota_cols.astype(np.float32))
    d3 = dict(
        ln1g_row=g["ln1_w"].reshape(1, D),
        ln1b_row=g["ln1_b"].reshape(1, D),
        ln2g_row=g["ln2_w"].reshape(1, D),
        ln2b_row=g["ln2_b"].reshape(1, D),
        m_in_w=g["m_in_w"],
        conv3_cols=np.ascontiguousarray(g["m_conv_w"][:, 3].reshape(16, 128).T),
        convb_cols=np.ascontiguousarray(g["m_conv_b"].reshape(16, 128).T),
        m_xproj=g["m_xproj_w"],
        m_out_w=g["m_out_w"],
        moe_r_w=g["moe_r_w"],
        moe_r_b_row=g["moe_r_b"].reshape(1, 8),
        moe_w1=g["moe_w1"],
        moe_b1_cols=np.ascontiguousarray(
            g["moe_b1"].reshape(8, 32, 128).transpose(2, 0, 1).reshape(128, 256)),
        moe_w2=g["moe_w2"],
        moe_b2=g["moe_b2"],
        identb=np.eye(128, dtype=BB),
        ones1=np.ones((1, 128), BB),
    )
    return d1, d2, d3
